# version 01
# baseline (speedup 1.0000x reference)
import math

import numpy as np
import ml_dtypes
import concourse.bass as bass
import concourse.mybir as mybir
from concourse.bass_utils import run_bass_kernel_spmd

F32 = mybir.dt.float32
BF16 = mybir.dt.bfloat16
AF = mybir.ActivationFunctionType
ALU = mybir.AluOpType
AX = mybir.AxisListType
BF = ml_dtypes.bfloat16

NCORES = 8
NT = 16
TPT = 128
TOK = NT * TPT
D = 1024
SEQ = 8192
NBLK = 64
LN_EPS = 1e-5
SCALE = 0.125



def shard_tokens(x_batches):
    out = []
    for c in range(NCORES):
        b, r = divmod(c, 4)
        xb = np.ascontiguousarray(x_batches[b]).reshape(NBLK, TPT, -1)
        out.append(np.ascontiguousarray(xb[r::4].reshape(TOK, -1)))
    return out


def unshard_tokens(slices, d, dtype=np.float32):
    x = np.empty((2, SEQ, d), dtype=dtype)
    for c in range(NCORES):
        b, r = divmod(c, 4)
        x[b].reshape(NBLK, TPT, d)[r::4] = np.asarray(slices[c]).reshape(NT, TPT, d)
    return x


def positions_for_core(c):
    b, r = divmod(c, 4)
    pos = []
    for t in range(r, NBLK, 4):
        pos.extend(range(t * TPT, (t + 1) * TPT))
    return np.array(pos, dtype=np.int64)


def deinterleave_perm(nh, dh=64):
    perm = []
    for h in range(nh):
        base = h * dh
        perm.extend([base + 2 * i for i in range(dh // 2)])
        perm.extend([base + 2 * i + 1 for i in range(dh // 2)])
    return np.array(perm, dtype=np.int64)


def swap_sign_matrix():
    P = np.zeros((128, 128), dtype=np.float32)
    for hb in range(2):
        b0 = hb * 64
        for i in range(32):
            P[b0 + 32 + i, b0 + i] = -1.0
            P[b0 + i, b0 + 32 + i] = 1.0
    return np.ascontiguousarray(P.astype(BF))


def causal_masks_for_core(r):
    ki = np.arange(TPT)
    out = np.zeros((16, TPT, 512), dtype=np.float32)
    for s in range(16):
        kpos = s * TPT + ki
        for i in range(4):
            qpos = (4 * i + r) * TPT + np.arange(TPT)
            out[s, :, i * TPT:(i + 1) * TPT] = (kpos[:, None] <= qpos[None, :])
    return np.ascontiguousarray(out.astype(BF))


def gidx(g):
    return (g % 4) * 16 + g // 4


def featmajor_rows(vec):
    return np.ascontiguousarray(np.asarray(vec, np.float32).reshape(-1, 128).T)


def gather_kt(kts):
    return np.ascontiguousarray(np.concatenate(kts, axis=2))


def gather_v_aug(vs):
    vw = vs[0].shape[1]
    nkh = vw // 64
    vcat = np.concatenate(vs, axis=0)
    out = np.zeros((nkh, NBLK, TPT, 66), dtype=BF)
    out[:, :, :, 64] = 1.0
    vr = vcat.reshape(NBLK, TPT, nkh, 64)
    for h in range(nkh):
        out[h, :, :, 0:64] = vr[:, :, h, :]
    return np.ascontiguousarray(out)


_IDENT = np.ascontiguousarray(np.eye(128, dtype=np.float32).astype(BF))




class Sem:
    def __init__(self, nc, name):
        self.sem = nc.alloc_semaphore(name)
        self.val = 0

    def inc(self, instr, n=1):
        instr.then_inc(self.sem, n)
        self.val += n
        return self.val

    def wait(self, eng, val=None):
        eng.wait_ge(self.sem, self.val if val is None else val)


class Prog:
    def __init__(self, name):
        self.nc = bass.Bass("TRN2", target_bir_lowering=False, name=name)
        self.input_names = []
        self.output_names = []
        self._semi = 0

    def din(self, name, shape, dt=F32):
        self.input_names.append(name)
        return self.nc.dram_tensor(name, list(shape), dt, kind="ExternalInput")

    def dout(self, name, shape, dt=F32):
        self.output_names.append(name)
        return self.nc.dram_tensor(name, list(shape), dt, kind="ExternalOutput")

    def sb(self, name, shape, dt):
        return self.nc.alloc_sbuf_tensor(name, list(shape), dt)

    def ps(self, name, shape, dt=F32):
        return self.nc.alloc_psum_tensor(name, list(shape), dt)

    def sem(self, name=None):
        self._semi += 1
        return Sem(self.nc, name or f"s{self._semi}")


def emit_load_x(p, xdram, X, sem, width=D):
    with p.nc.Block() as b:
        @b.sync
        def _(e):
            src = xdram.ap().rearrange("(t p) d -> p t d", p=TPT)
            dst = X.ap().rearrange("p (t d) -> p t d", d=width)
            sem.inc(e.dma_start(dst, src), 16)
            sem.wait(e)


def emit_store_x(p, X, xdram, sem, width=D):
    with p.nc.Block() as b:
        @b.sync
        def _(e):
            dst = xdram.ap().rearrange("(t p) d -> p t d", p=TPT)
            src = X.ap().rearrange("p (t d) -> p t d", d=width)
            sem.inc(e.dma_start(dst, src), 16)
            sem.wait(e)


def emit_ln_stats(p, X, mu, rstd, scr, ntiles=NT, width=D, eps=LN_EPS, rms=False):
    nc = p.nc
    with nc.Block() as b:
        @b.vector
        def _(e):
            for t in range(ntiles):
                xt = X[:, t * width:(t + 1) * width]
                if not rms:
                    e.reduce_sum(mu[:, t:t + 1], xt, axis=AX.X)
                e.scalar_tensor_tensor(scr[:, 0:width], xt, 1.0, xt,
                                       ALU.mult, ALU.mult,
                                       accum_out=rstd[:, t:t + 1])
    with nc.Block() as b:
        @b.vector
        def _(e):
            e.tensor_scalar_mul(rstd[:], rstd[:], 1.0 / width)
            if not rms:
                e.tensor_scalar_mul(mu[:], mu[:], 1.0 / width)
                e.tensor_tensor(scr[:, 0:ntiles], mu[:], mu[:], ALU.mult)
                e.tensor_sub(rstd[:], rstd[:], scr[:, 0:ntiles])
            else:
                e.memset(mu[:], 0.0)
            e.tensor_scalar_add(rstd[:], rstd[:], float(eps))
    with nc.Block() as b:
        @b.scalar
        def _(e):
            e.activation(scr[:, 0:ntiles], rstd[:], AF.Ln)
            e.activation(rstd[:], scr[:, 0:ntiles], AF.Exp, scale=-0.5)


def emit_normalize(p, X, XH, mu, rstd, ntiles=NT, width=D):
    with p.nc.Block() as b:
        @b.vector
        def _(e):
            for t in range(ntiles):
                e.tensor_scalar(XH[:, t * width:(t + 1) * width],
                                X[:, t * width:(t + 1) * width],
                                mu[:, t:t + 1], rstd[:, t:t + 1],
                                ALU.subtract, ALU.mult)


def emit_cast(p, X, XH, ntiles=NT, width=D):
    with p.nc.Block() as b:
        @b.vector
        def _(e):
            for t in range(ntiles):
                e.tensor_copy(XH[:, t * width:(t + 1) * width],
                              X[:, t * width:(t + 1) * width])


def emit_transpose_fused(p, Xsrc, XT, idsb, psT, stg, smm, sev, mu=None,
                         rstd=None, nchunks=8, ntiles=NT, tok=TOK, width=None):
    width = width or (nchunks * 128)
    nc = p.nc
    groups = []
    for t in range(ntiles):
        for f0 in range(0, nchunks, 4):
            groups.append((t, f0, min(4, nchunks - f0)))
    G = len(groups)
    order = []
    for g in range(G):
        order.append(("c", g))
        if g >= 2:
            order.append(("e", g - 2))
    order.append(("e", G - 2))
    order.append(("e", G - 1))
    val = {}
    for i, op in enumerate(order):
        val[op] = i + 1
    with nc.Block() as b:
        mmb = smm.val
        evb = sev.val

        @b.vector
        def _(e):
            for kind, g in order:
                t, f0, nf = groups[g]
                offS = (g % 2) * 512
                offT = (g % 2) * 1024
                if kind == "c":
                    if g >= 2:
                        e.wait_ge(smm.sem, mmb + g - 1)
                    sl = stg[:, offS:offS + nf * 128]
                    xsl = Xsrc[:, t * width + f0 * 128: t * width + (f0 + nf) * 128]
                    if mu is None:
                        ins = e.tensor_copy(sl, xsl)
                    else:
                        ins = e.tensor_scalar(sl, xsl, mu[:, t:t + 1],
                                              rstd[:, t:t + 1],
                                              ALU.subtract, ALU.mult)
                else:
                    e.wait_ge(smm.sem, mmb + g + 1)
                    for j in range(nf):
                        f = f0 + j
                        ins = e.tensor_copy(
                            XT[:, f * tok + t * 128: f * tok + t * 128 + 128],
                            psT[:, offT + j * 128:offT + (j + 1) * 128])
                sev.inc(ins, 1)

        @b.tensor
        def _(e):
            for g in range(G):
                t, f0, nf = groups[g]
                offS = (g % 2) * 512
                offT = (g % 2) * 1024
                e.wait_ge(sev.sem, evb + val[("c", g)])
                if g >= 2:
                    e.wait_ge(sev.sem, evb + val[("e", g - 2)])
                for j in range(nf):
                    ins = e.transpose(psT[:, offT + j * 128:offT + (j + 1) * 128],
                                      stg[:, offS + j * 128:offS + (j + 1) * 128],
                                      idsb[:])
                smm.inc(ins, 1)


def emit_transpose_xt(p, XH, XT, idsb, psT, smm, sev, nchunks=8, ntiles=NT,
                      tok=TOK, width=None):
    width = width or (nchunks * 128)
    nc = p.nc
    pairs = [(t * width + f * 128, f * tok + t * 128)
             for t in range(ntiles) for f in range(nchunks)]
    groups = [pairs[i:i + 4] for i in range(0, len(pairs), 4)]
    with nc.Block() as b:
        mmb = smm.val
        evb = sev.val

        @b.tensor
        def _(e):
            for gi, grp in enumerate(groups):
                off = (gi % 2) * 1024
                if gi >= 2:
                    e.wait_ge(sev.sem, evb + gi - 1)
                for j, (sc, dc) in enumerate(grp):
                    ins = e.transpose(psT[:, off + j * 128:off + (j + 1) * 128],
                                      XH[:, sc:sc + 128], idsb[:])
                smm.inc(ins, 1)

        @b.vector
        def _(e):
            for gi, grp in enumerate(groups):
                off = (gi % 2) * 1024
                e.wait_ge(smm.sem, mmb + gi + 1)
                for j, (sc, dc) in enumerate(grp):
                    ins = e.tensor_copy(XT[:, dc:dc + 128],
                                        psT[:, off + j * 128:off + (j + 1) * 128])
                sev.inc(ins, 1)


def emit_wload(e, wdram, rows, col0, ncols, wbuf, sem, row0=0):
    nch = rows // 128
    src = wdram.ap()[row0:row0 + rows, col0:col0 + ncols] \
        .rearrange("(c p) n -> p c n", p=128)
    dst = wbuf.ap().rearrange("p (c n) -> p c n", n=ncols)[:, 0:nch, :]
    sem.inc(e.dma_start(dst, src), 16)




def build_qkv(name, *, preln, affine, qw, kw, vw, rope, compress=None,
              qcompress=None):
    p = Prog(name)
    nc = p.nc
    xdram = p.din("x", [TOK, D])
    qt_out = p.dout("qt", [qw // 128, 128, TOK], BF16)
    kt_out = p.dout("kt", [max(kw // 128, 1), 128 if kw >= 128 else kw, TOK], BF16)
    v_out = p.dout("v", [TOK, vw], BF16)
    qkn_out = p.dout("qkn", [1, 32])

    if qcompress is None:
        wq = p.din("wq", [D, qw])
    else:
        qcw = qcompress["cw"]
        wq_c = p.din("wq_c", [D, qcw])
        wq2 = p.din("wq2", [qcw, qw])
    if compress is None:
        wk = p.din("wk", [D, kw])
        wv = p.din("wv", [D, vw])
    else:
        cw = compress["cw"]
        wk_c = p.din("wk_c", [D, cw])
        wk2 = p.din("wk2", [cw, kw])
        wv_c = p.din("wv_c", [D, cw])
        wv2 = p.din("wv2", [cw, vw])

    ident = p.din("ident", [128, 128], BF16)
    if affine:
        gam = p.din("gam", [128, 8])
        bet = p.din("bet", [128, 8])
    if rope:
        ctab = p.din("ctab", [128, TOK], BF16)
        stab = p.din("stab", [128, TOK], BF16)
        swp = p.din("swp", [128, 128], BF16)

    X = p.sb("X", [128, NT * D], F32)
    XT = p.sb("XT", [128, 8 * TOK], BF16)
    idsb = p.sb("idsb", [128, 128], BF16)
    scr = p.sb("scr", [128, D], F32)
    stg = p.sb("stg", [128, 1024], BF16)
    sload = p.sem("sload")
    smm = p.sem("smm")
    sev = p.sem("sev")
    swt = p.sem("swt")
    sst = p.sem("sst")

    psT = p.ps("psT", [128, 2048], BF16)
    psP = p.ps("psP", [128, 1024], F32)
    psR = p.ps("psR", [128, 2048], F32)

    with nc.Block() as b:
        @b.sync
        def _(e):
            sload.inc(e.dma_start(idsb[:], ident[:]), 16)
    emit_load_x(p, xdram, X, sload)

    if preln:
        mu = p.sb("mu", [128, NT], F32)
        rstd = p.sb("rstd", [128, NT], F32)
        emit_ln_stats(p, X, mu, rstd, scr)
        emit_transpose_fused(p, X, XT, idsb, psT, stg, smm, sev, mu=mu, rstd=rstd)
    else:
        emit_transpose_fused(p, X, XT, idsb, psT, stg, smm, sev)

    if affine:
        gsb = p.sb("gsb", [128, 8], F32)
        bsb = p.sb("bsb", [128, 8], F32)
        with nc.Block() as b:
            @b.sync
            def _(e):
                sload.inc(e.dma_start(gsb[:], gam[:]), 16)
                sload.inc(e.dma_start(bsb[:], bet[:]), 16)
                sload.wait(e)
        with nc.Block() as b:
            @b.vector
            def _(e):
                for f in range(8):
                    e.tensor_scalar(XT[:, f * TOK:(f + 1) * TOK],
                                    XT[:, f * TOK:(f + 1) * TOK],
                                    gsb[:, f:f + 1], bsb[:, f:f + 1],
                                    ALU.mult, ALU.add)

    if rope:
        csb = p.sb("csb", [128, TOK], BF16)
        ssb = p.sb("ssb", [128, TOK], BF16)
        swsb = p.sb("swsb", [128, 128], BF16)
        with nc.Block() as b:
            @b.sync
            def _(e):
                sload.inc(e.dma_start(csb[:], ctab[:]), 16)
                sload.inc(e.dma_start(ssb[:], stab[:]), 16)
                sload.inc(e.dma_start(swsb[:], swp[:]), 16)
                sload.wait(e)

    wbuf = p.sb("wbuf", [128, 8 * 512], BF16)
    cbuf = p.sb("cbuf", [128, TOK], BF16)
    rsc = p.sb("rsc", [128, 512], BF16)
    nrm = p.sb("nrm", [128, 1], F32)
    nrm2 = p.sb("nrm2", [128, 1], F32)
    qknsb = p.sb("qknsb", [1, 32], F32)
    ones64f = p.sb("ones64f", [128, 1], F32)
    vtile = p.sb("vtile", [128, 512], BF16)
    with nc.Block() as b:
        @b.vector
        def _(e):
            e.memset(ones64f[:], 1.0)
            e.memset(qknsb[:], 0.0)

    def fm_chunk(wdram, src_XT, src_nch, mchunk, dst_sb, dst_col0):
        with nc.Block() as b:
            @b.gpsimd
            def _(e):
                emit_wload(e, wdram, src_nch * 128, mchunk * 128, 128, wbuf, swt)

            @b.tensor
            def _(e):
                swt.wait(e)
                evb = sev.val
                for tg in range(4):
                    ps = psP[:, (tg % 2) * 512:(tg % 2) * 512 + 512]
                    if tg >= 2:
                        e.wait_ge(sev.sem, evb + tg - 1)
                    for dc in range(src_nch):
                        ins = e.matmul(
                            ps, wbuf[:, dc * 128:(dc + 1) * 128],
                            src_XT[:, dc * TOK + tg * 512: dc * TOK + tg * 512 + 512],
                            start=(dc == 0), stop=(dc == src_nch - 1))
                    smm.inc(ins, 1)

            @b.vector
            def _(e):
                mmb = smm.val - 4
                for tg in range(4):
                    ps = psP[:, (tg % 2) * 512:(tg % 2) * 512 + 512]
                    e.wait_ge(smm.sem, mmb + tg + 1)
                    sev.inc(e.tensor_copy(
                        dst_sb[:, dst_col0 + tg * 512: dst_col0 + tg * 512 + 512],
                        ps), 1)

    def rope_chunk(buf, col0):
        with nc.Block() as b:
            @b.tensor
            def _(e):
                for tg in range(4):
                    smm.inc(e.matmul(psR[:, tg * 512:(tg + 1) * 512], swsb[:],
                                     buf[:, col0 + tg * 512:col0 + (tg + 1) * 512],
                                     start=True, stop=True), 1)

            @b.vector
            def _(e):
                mmb = smm.val - 4
                for tg in range(4):
                    sl = slice(col0 + tg * 512, col0 + (tg + 1) * 512)
                    tsl = slice(tg * 512, (tg + 1) * 512)
                    e.wait_ge(smm.sem, mmb + tg + 1)
                    e.tensor_tensor(rsc[:], psR[:, tsl], ssb[:, tsl], ALU.mult)
                    e.tensor_tensor(buf[:, sl], buf[:, sl], csb[:, tsl], ALU.mult)
                    e.tensor_add(buf[:, sl], buf[:, sl], rsc[:])

    def norms_chunk(buf, col0, norm_col):
        with nc.Block() as b:
            mmb = smm.val
            evb = sev.val

            @b.vector
            def _(e):
                for half in range(2):
                    if half == 1:
                        e.wait_ge(smm.sem, mmb + 4)
                    sl = slice(col0 + half * 1024, col0 + half * 1024 + 1024)
                    e.tensor_tensor(scr[:], buf[:, sl], buf[:, sl], ALU.mult)
                    sev.inc(e.engine_nop(), 1)

            @b.tensor
            def _(e):
                for half in range(2):
                    e.wait_ge(sev.sem, evb + half + 1)
                    for hh in range(2):
                        for g in range(2):
                            ins = e.matmul(
                                psR[0:1, (hh * 2 + g) * 512:(hh * 2 + g) * 512 + 512],
                                ones64f[hh * 64:(hh + 1) * 64, 0:1],
                                scr[hh * 64:(hh + 1) * 64, g * 512:(g + 1) * 512],
                                start=(half == 0), stop=(half == 1))
                            if half == 0 and hh == 1 and g == 1:
                                smm.inc(ins, 4)
                    if half == 1:
                        smm.inc(ins, 1)

            @b.vector
            def _(e):
                smm.wait(e)
                for hh in range(2):
                    e.reduce_max(qknsb[0:1, norm_col + hh:norm_col + hh + 1],
                                 psR[0:1, hh * 1024:hh * 1024 + 1024], axis=AX.X)

    def proj_featmajor(wdram, width, do_rope, norm_col0, out_dram,
                       src_XT=None, src_nch=8):
        src_XT = XT if src_XT is None else src_XT
        for mchunk in range(width // 128):
            fm_chunk(wdram, src_XT, src_nch, mchunk, cbuf, 0)
            if do_rope:
                rope_chunk(cbuf, 0)
            norms_chunk(cbuf, 0, norm_col0 + mchunk * 2)
            with nc.Block() as b:
                @b.sync
                def _(e):
                    sst.inc(e.dma_start(out_dram.ap()[mchunk], cbuf[:]), 16)
                    sst.wait(e)

    def proj_tokmajor(wdram, width, out_dram, src_XT=None, src_nch=8):
        src_XT = XT if src_XT is None else src_XT
        for vg in range((width + 511) // 512):
            vn = min(512, width - vg * 512)
            with nc.Block() as b:
                @b.gpsimd
                def _(e):
                    emit_wload(e, wdram, src_nch * 128, vg * 512, vn, wbuf, swt)
            for t in range(NT):
                with nc.Block() as b:
                    @b.tensor
                    def _(e):
                        swt.wait(e)
                        for dc in range(src_nch):
                            ins = e.matmul(
                                psP[:, 0:vn],
                                src_XT[:, dc * TOK + t * 128: dc * TOK + t * 128 + 128],
                                wbuf[:, dc * vn:(dc + 1) * vn],
                                start=(dc == 0), stop=(dc == src_nch - 1))
                        smm.inc(ins, 1)

                    @b.vector
                    def _(e):
                        e.wait_ge(sst.sem, sst.val)
                        smm.wait(e)
                        sev.inc(e.tensor_copy(vtile[:, 0:vn], psP[:, 0:vn]), 1)

                    @b.sync
                    def _(e):
                        sev.wait(e)
                        dst = out_dram.ap()[t * 128:(t + 1) * 128,
                                            vg * 512:vg * 512 + vn]
                        sst.inc(e.dma_start(dst, vtile[:, 0:vn]), 16)

    def stage2_to_sbuf(wdram, width, dst_sb):
        for mchunk in range(width // 128):
            fm_chunk(wdram, XT, 8, mchunk, dst_sb, mchunk * TOK)

    if qcompress is None:
        proj_featmajor(wq, qw, bool(rope), 0, qt_out)
    else:
        qcw = qcompress["cw"]
        cqT = p.sb("cqT", [128, (qcw // 128) * TOK], BF16)
        stage2_to_sbuf(wq_c, qcw, cqT)
        proj_featmajor(wq2, qw, bool(rope), 0, qt_out,
                       src_XT=cqT, src_nch=qcw // 128)
    if compress is None:
        proj_featmajor(wk, kw, bool(rope), 16, kt_out)
        proj_tokmajor(wv, vw, v_out)
    else:
        cw = compress["cw"]
        ckT = p.sb("ckT", [128, (cw // 128) * TOK], BF16)
        stage2_to_sbuf(wk_c, cw, ckT)
        proj_featmajor(wk2, kw, bool(rope), 16, kt_out,
                       src_XT=ckT, src_nch=cw // 128)
        cvT = p.sb("cvT", [128, (cw // 128) * TOK], BF16)
        stage2_to_sbuf(wv_c, cw, cvT)
        proj_tokmajor(wv2, vw, v_out, src_XT=cvT, src_nch=cw // 128)

    with nc.Block() as b:
        @b.sync
        def _(e):
            sst.wait(e)
            sload.inc(e.dma_start(qkn_out[:], qknsb[:]), 16)
            sload.wait(e)
    return p




def build_attn_core(name, *, nqh, nkh, causal, res_mult, dbg=False,
                    dbg_hp_limit=None, dbg_ncks=None, tail="postln",
                    dbgffn=False):
    half_s = (tail == "l3ffn")
    """Inputs: x, qt [qwch,128,2048] bf16, ktg [kch,128,8192] bf16,
    vg [nkh,64,128,65] bf16, cbt [1,128] f32, wo [nqh*64,1024] f32,
    masks (causal). Output: xo = LN(res_mult*x + attn@wo) fp32."""
    qwch = nqh * 64 // 128
    kch = max(nkh // 2, 1)
    p = Prog(name)
    nc = p.nc
    xdram = p.din("x", [TOK, D])
    qt = p.din("qt", [qwch, 128, TOK], BF16)
    ktg = p.din("ktg", [kch, 128, SEQ], BF16)
    vg = p.din("vg", [nkh, NBLK, 128, 66], BF16)
    cbt = p.din("cbt", [1, 128])
    wo = p.din("wo", [nqh * 64, D])
    if causal:
        masks = p.din("masks", [16, 128, 512], BF16)
    if tail == "l3ffn":
        n1r = p.din("n1r", [1, D])
        n2r = p.din("n2r", [1, D])
        wgwe = p.din("wgwe", [D, 16])
        gebias = p.din("gebias", [1, 16])
        wsi = p.din("wsi", [D, 4096])
        wso = p.din("wso", [2048, D])
        w1s = p.din("w1s", [D, 4096])
        wexp = p.din("wexp", [8, 2048, D])
        ident = p.din("ident", [128, 128], BF16)
    xo = p.dout("xo", [TOK, D])
    if dbg:
        dbg_at = p.dout("dbg_at", [nqh * 64 // 128, 128, TOK], BF16)
        dbg_cbc = p.dout("dbg_cbc", [128, 16])
        dbg_pt = p.dout("dbg_pt", [128, 2048], BF16)
        dbg_ps = p.dout("dbg_ps", [128, 2048])
        dbg_vb = p.dout("dbg_vb", [128, 2 * NBLK * 66], BF16)
        dbg_rb = p.dout("dbg_rb", [64, 1024])
        dbg_po = p.dout("dbg_po", [128, 1024])
        dbg_x2 = p.dout("dbg_x2", [TOK, D])
        dbg_st = p.dout("dbg_st", [128, 2 * NT])

    sload = p.sem("sload")
    skv = p.sem("skv")
    smm = p.sem("smm")
    sex = p.sem("sex")
    sev = p.sem("sev")
    spv = p.sem("spv")
    sst = p.sem("sst")

    from contextlib import ExitStack
    X = p.sb("X", [128, NT * D], F32)
    cbc = p.sb("cbc", [128, 16], F32)
    cb_row = p.sb("cb_row", [1, 128], F32)
    qm = p.sb("qm", [1, 16], F32)
    km = p.sb("km", [1, 16], F32)
    recl = p.sb("recl", [128, 1024], F32)
    rb = p.sb("rb", [64, 1024], F32)
    stg2 = p.sb("stg2", [64, 512], BF16)
    reclb = p.sb("reclb", [1, 1024], BF16)
    ones1f = p.sb("ones1f", [1, 128], F32)
    ones1b = p.sb("ones1b", [1, 128], BF16)
    scr = p.sb("scr", [128, D], F32)
    mu = p.sb("mu", [128, NT], F32)
    rstd = p.sb("rstd", [128, NT], F32)
    stkA = ExitStack()
    AT = stkA.enter_context(nc.sbuf_tensor("AT", [128, qwch * TOK], BF16))
    KT = stkA.enter_context(nc.sbuf_tensor("KT", [128, SEQ], BF16))
    VB = stkA.enter_context(nc.sbuf_tensor("VB", [128, 2 * NBLK * 66], BF16))
    QTb = stkA.enter_context(nc.sbuf_tensor("QTb", [128, TOK], BF16))
    PT = stkA.enter_context(nc.sbuf_tensor("PT", [128, 2048], BF16))
    wosb = stkA.enter_context(nc.sbuf_tensor("wosb", [128, qwch * D], BF16))
    msksb = stkA.enter_context(nc.sbuf_tensor("msksb", [128, 16 * 512], BF16)) \
        if causal else None

    if not half_s:
        psS = p.ps("psS", [128, 2048], F32)
    else:
        psS = p.ps("psS", [128, 1024], F32)
    psO = p.ps("psO", [128, 1024], F32)
    psB = p.ps("psB", [128, 1024], F32)
    if half_s:
        psT = p.ps("psT", [128, 2048], BF16)

    with nc.Block() as b:
        @b.sync
        def _(e):
            sload.inc(e.dma_start(cb_row[:], cbt[:]), 16)
            if causal:
                sload.inc(e.dma_start(
                    msksb.ap().rearrange("p (s q) -> p s q", q=512),
                    masks.ap().rearrange("s p q -> p s q")), 16)
            sload.wait(e)

        @b.vector
        def _(e):
            e.memset(ones1f[:], 1.0)
            e.memset(ones1b[:], 1.0)

    with nc.Block() as b:
        @b.vector
        def _(e):
            e.tensor_max(qm[:], cb_row[0:1, 0:16], cb_row[0:1, 32:48])
            e.tensor_max(qm[:], qm[:], cb_row[0:1, 64:80])
            e.tensor_max(qm[:], qm[:], cb_row[0:1, 96:112])
            e.tensor_max(km[:], cb_row[0:1, 16:32], cb_row[0:1, 48:64])
            e.tensor_max(km[:], km[:], cb_row[0:1, 80:96])
            e.tensor_max(km[:], km[:], cb_row[0:1, 112:128])
            if nkh < nqh:
                g = nqh // nkh
                kme = bass.AP(km.ap().tensor, km.ap().offset,
                              [[16, 1], [1, nkh], [0, g]])
                qme = qm.ap().rearrange("p (a b) -> p a b", b=g)
                e.tensor_tensor(qme, qme, kme, ALU.mult)
            else:
                e.tensor_tensor(qm[:], qm[:], km[:], ALU.mult)
    with nc.Block() as b:
        @b.scalar
        def _(e):
            e.activation(km[:], qm[:], AF.Ln)
            e.activation(qm[:], km[:], AF.Exp, scale=0.5)
    with nc.Block() as b:
        @b.vector
        def _(e):
            e.tensor_scalar_mul(qm[:], qm[:], -SCALE)
    with nc.Block() as b:
        @b.tensor
        def _(e):
            smm.inc(e.matmul(psB[:, 0:16], ones1f[:], qm[:],
                             start=True, stop=True), 1)

        @b.vector
        def _(e):
            smm.wait(e)
            e.tensor_copy(cbc[:], psB[:, 0:16])

    kvg = nqh // nkh
    nhp = nqh // 2 if dbg_hp_limit is None else dbg_hp_limit
    for hp in range(nhp):
        h1, h2 = 2 * hp, 2 * hp + 1
        kv1, kv2 = h1 // kvg, h2 // kvg
        with nc.Block() as b:
            @b.sync
            def _(e):
                for i, kv in enumerate((kv1, kv2)):
                    src = ktg.ap()[kv // 2, (kv % 2) * 64:(kv % 2) * 64 + 64, :]
                    skv.inc(e.dma_start(KT[i * 64:(i + 1) * 64, :], src), 16)
                    vsrc = vg.ap()[kv].rearrange("c p j -> p c j")
                    vdst = VB.ap()[:, i * NBLK * 66:(i + 1) * NBLK * 66] \
                        .rearrange("p (c j) -> p c j", j=66)
                    skv.inc(e.dma_start(vdst, vsrc), 16)
                src = qt.ap()[hp]
                skv.inc(e.dma_start(QTb[:], src), 16)
        for u in range(4 if dbg_ncks is None else 1):
            ncks = ((16 * u + 16) if causal else NBLK) if dbg_ncks is None else dbg_ncks
            PAIR = 1 if half_s else 2
            npair = (ncks + PAIR - 1) // PAIR
            with nc.Block() as b:
                mmb = smm.val
                exb = sex.val
                pvb = spv.val
                mkb = sev.val

                def s_cols(hi, j):
                    if half_s:
                        return slice(hi * 512, hi * 512 + 512)
                    return slice(hi * 1024 + j * 512, hi * 1024 + j * 512 + 512)

                @b.tensor
                def _(e):
                    skv.wait(e)
                    for pi in range(npair + 1):
                        if pi < npair:
                            for hi in range(2):
                                if pi >= 1:
                                    e.wait_ge(sex.sem, exb + (pi - 1) * 2 + hi + 1)
                                for j in range(PAIR):
                                    ci = pi * PAIR + j
                                    if ci >= ncks:
                                        continue
                                    gg = gidx(ci) if causal else ci
                                    ins = e.matmul(
                                        psS[:, s_cols(hi, j)],
                                        KT[hi * 64:(hi + 1) * 64,
                                           gg * 128:gg * 128 + 128],
                                        QTb[hi * 64:(hi + 1) * 64,
                                            u * 512:u * 512 + 512],
                                        start=True, stop=True)
                                smm.inc(ins, 1)
                        if pi >= 1:
                            ppi = pi - 1
                            for hi in range(2):
                                if causal:
                                    e.wait_ge(sev.sem, mkb + ppi * 2 + hi + 1)
                                else:
                                    e.wait_ge(sex.sem, exb + ppi * 2 + hi + 1)
                                for j in range(PAIR):
                                    ci = ppi * PAIR + j
                                    if ci >= ncks:
                                        continue
                                    gg = gidx(ci) if causal else ci
                                    ins = e.matmul(
                                        psO[0:66, hi * 512:hi * 512 + 512],
                                        VB[:, (hi * NBLK + gg) * 66:
                                           (hi * NBLK + gg) * 66 + 66],
                                        PT[:, s_cols(hi, j)],
                                        start=(ci == 0), stop=(ci == ncks - 1))
                                spv.inc(ins, 1)

                @b.scalar
                def _(e):
                    for pi in range(npair):
                        if half_s:
                            wid = 512
                        else:
                            wid = 1024 if (pi * 2 + 1 < ncks) else 512
                        for hi, h in enumerate((h1, h2)):
                            e.wait_ge(smm.sem, mmb + pi * 2 + hi + 1)
                            if pi >= 1:
                                e.wait_ge(spv.sem, pvb + (pi - 1) * 2 + hi + 1)
                            c0 = (hi * 512) if half_s else (hi * 1024)
                            sex.inc(e.activation(
                                PT[:, c0:c0 + wid],
                                psS[:, c0:c0 + wid],
                                AF.Exp, scale=SCALE, bias=cbc[:, h:h + 1]), 1)

                if causal:
                    @b.vector
                    def _(e):
                        for pi in range(npair):
                            for hi in range(2):
                                e.wait_ge(sex.sem, exb + pi * 2 + hi + 1)
                                ins = None
                                for j in range(PAIR):
                                    ci = pi * PAIR + j
                                    if ci >= ncks:
                                        continue
                                    s = ci - 16 * u
                                    if s >= 0:
                                        ins = e.tensor_tensor(
                                            PT[:, s_cols(hi, j)],
                                            PT[:, s_cols(hi, j)],
                                            msksb[:, s * 512:(s + 1) * 512],
                                            ALU.mult)
                                if ins is None:
                                    ins = e.engine_nop()
                                sev.inc(ins, 1)
            with nc.Block() as b:
                evb = sev.val

                @b.vector
                def _(e):
                    spv.wait(e)
                    ins = e.reciprocal(recl[64:65, 0:1024], psO[64:65, 0:1024])
                    sev.inc(ins, 1)

                @b.sync
                def _(e):
                    sev.wait(e)
                    sst.inc(e.dma_start(recl[0:1, 0:1024], recl[64:65, 0:1024]), 16)

                @b.vector
                def _(e):
                    sst.wait(e)
                    sev.inc(e.tensor_copy(reclb[:], recl[0:1, 0:1024]), 1)

                @b.tensor
                def _(e):
                    sev.wait(e)
                    for hi in range(2):
                        smm.inc(e.matmul(psB[0:64, hi * 512:hi * 512 + 512],
                                         ones1b[0:1, 0:64],
                                         reclb[0:1, hi * 512:hi * 512 + 512],
                                         start=True, stop=True), 1)

                @b.vector
                def _(e):
                    smm.wait(e)
                    e.tensor_copy(rb[:], psB[0:64, :])
                    dst1 = AT[0:64, (h1 // 2) * TOK + u * 512:
                              (h1 // 2) * TOK + u * 512 + 512]
                    e.tensor_tensor(dst1, psO[0:64, 0:512], rb[0:64, 0:512],
                                    ALU.mult)
                    sev.inc(e.tensor_tensor(stg2[0:64, :], psO[0:64, 512:1024],
                                            rb[0:64, 512:1024], ALU.mult), 1)

                @b.sync
                def _(e):
                    sev.wait(e)
                    dst2 = AT[64:128, (h2 // 2) * TOK + u * 512:
                              (h2 // 2) * TOK + u * 512 + 512]
                    sst.inc(e.dma_start(dst2, stg2[0:64, :]), 16)
                    sst.wait(e)

    if dbg:
        dbg_po_sb = p.sb("dbg_po_sb", [128, 1024], F32)
        with nc.Block() as b:
            @b.vector
            def _(e):
                e.tensor_copy(recl[:, 0:1024], psS[:, 0:1024])
                e.tensor_copy(dbg_po_sb[0:66, :], psO[0:66, :])

            @b.sync
            def _(e):
                sload.inc(e.dma_start(
                    dbg_at.ap().rearrange("c p t -> p c t"),
                    AT.ap().rearrange("p (c t) -> p c t", t=TOK)), 16)
                sload.inc(e.dma_start(dbg_cbc.ap(), cbc[:]), 16)
                sload.inc(e.dma_start(dbg_pt.ap(), PT[:]), 16)
                sload.inc(e.dma_start(dbg_ps.ap()[:, 0:1024], recl[:, 0:1024]), 16)
                sload.inc(e.dma_start(dbg_vb.ap(), VB[:]), 16)
                sload.inc(e.dma_start(dbg_rb.ap(), rb[:]), 16)
                sload.inc(e.dma_start(dbg_po.ap(), dbg_po_sb[:]), 16)
                sload.wait(e)

    emit_load_x(p, xdram, X, sload)
    with nc.Block() as b:
        @b.gpsimd
        def _(e):
            emit_wload(e, wo, qwch * 128, 0, D, wosb, sload)

        @b.sync
        def _(e):
            sload.wait(e)
    for t in range(NT):
        with nc.Block() as b:
            @b.tensor
            def _(e):
                for half in range(2):
                    for fc in range(qwch):
                        ins = e.matmul(
                            psS[:, half * 512:half * 512 + 512],
                            AT[:, fc * TOK + t * 128:fc * TOK + t * 128 + 128],
                            wosb[:, fc * D + half * 512:fc * D + half * 512 + 512],
                            start=(fc == 0), stop=(fc == qwch - 1))
                    smm.inc(ins, 1)

            @b.vector
            def _(e):
                mmb = smm.val - 2
                for half in range(2):
                    e.wait_ge(smm.sem, mmb + half + 1)
                    sl = slice(t * D + half * 512, t * D + half * 512 + 512)
                    e.scalar_tensor_tensor(X[:, sl], X[:, sl], float(res_mult),
                                           psS[:, half * 512:half * 512 + 512],
                                           ALU.mult, ALU.add)
    if dbg:
        with nc.Block() as b:
            @b.sync
            def _(e):
                sload.inc(e.dma_start(dbg_x2.ap().rearrange("(t p) d -> p t d", p=TPT),
                                      X.ap().rearrange("p (t d) -> p t d", d=D)), 16)
                sload.wait(e)
    if tail == "l3ffn":
        stkA.close()
        return _emit_l3ffn_tail(p, dict(
            X=X, xo=xo, xdram=xdram, scr=scr, mu=mu, rstd=rstd,
            sload=sload, skv=skv, smm=smm, sex=sex, sev=sev, sst=sst,
            psS=psS, psO=psO, psB=psB, psT=psT, ones1f=ones1f,
            n1r=n1r, n2r=n2r, wgwe=wgwe, gebias=gebias, wsi=wsi,
            wso=wso, w1s=w1s, wexp=wexp, ident=ident, dbgffn=dbgffn))
    emit_ln_stats(p, X, mu, rstd, scr)
    if dbg:
        with nc.Block() as b:
            @b.sync
            def _(e):
                sload.inc(e.dma_start(dbg_st.ap()[:, 0:NT], mu[:]), 16)
                sload.inc(e.dma_start(dbg_st.ap()[:, NT:2 * NT], rstd[:]), 16)
                sload.wait(e)
    return _finish_attn_store(p, X, mu, rstd, xo, sst)


def _finish_attn_store(p, X, mu, rstd, xo, sst):
    nc = p.nc
    with nc.Block() as b:
        @b.vector
        def _(e):
            for t in range(NT):
                e.tensor_scalar(X[:, t * D:(t + 1) * D], X[:, t * D:(t + 1) * D],
                                mu[:, t:t + 1], rstd[:, t:t + 1],
                                ALU.subtract, ALU.mult)
    emit_store_x(p, X, xo, sst)
    return p


def emit_bcast_row(p, rowsb, col0, ncols, dst_sb, dst_col0, ones1f, psX, smm):
    nc = p.nc
    with nc.Block() as b:
        @b.tensor
        def _(e):
            for i in range((ncols + 511) // 512):
                w = min(512, ncols - i * 512)
                smm.inc(e.matmul(psX[:, i * 512:i * 512 + w], ones1f[0:1, :],
                                 rowsb[0:1, col0 + i * 512:col0 + i * 512 + w],
                                 start=True, stop=True), 1)

        @b.vector
        def _(e):
            smm.wait(e)
            e.tensor_copy(dst_sb[:, dst_col0:dst_col0 + ncols],
                          psX[:, 0:ncols])


def emit_postln_store(p, X, xo, scr, sst, rms=False, wtile=None):
    mu = p.sb("pl_mu", [128, NT], F32)
    rstd = p.sb("pl_rstd", [128, NT], F32)
    emit_ln_stats(p, X, mu, rstd, scr, rms=rms, eps=(1e-6 if rms else LN_EPS))
    with p.nc.Block() as b:
        @b.vector
        def _(e):
            for t in range(NT):
                e.tensor_scalar(X[:, t * D:(t + 1) * D], X[:, t * D:(t + 1) * D],
                                mu[:, t:t + 1], rstd[:, t:t + 1],
                                ALU.subtract, ALU.mult)
                if wtile is not None:
                    e.tensor_tensor(X[:, t * D:(t + 1) * D],
                                    X[:, t * D:(t + 1) * D], wtile[:], ALU.mult)
    emit_store_x(p, X, xo, sst)
    return p


def build_l0_ffn(name):
    p = Prog(name)
    nc = p.nc
    xdram = p.din("x", [TOK, D])
    win = p.din("win", [D, 64])
    wout = p.din("wout", [64, D])
    binr = p.din("binr", [1, 64])
    w1r = p.din("w1r", [1, 1024])
    b1r = p.din("b1r", [1, 1024])
    w2r = p.din("w2r", [1, 1024])
    b2c = p.din("b2c", [64, 1])
    boutr = p.din("boutr", [1, D])
    ident = p.din("ident", [128, 128], BF16)
    xo = p.dout("xo", [TOK, D])

    X = p.sb("X", [128, NT * D], F32)
    XT = p.sb("XT", [128, 8 * TOK], BF16)
    idsb = p.sb("idsb", [128, 128], BF16)
    scr = p.sb("scr", [128, D], F32)
    scr2 = p.sb("scr2", [128, 512], F32)
    stg = p.sb("stg", [128, 1024], BF16)
    rows = p.sb("rows", [1, 4 * 1024], F32)
    boutsb = p.sb("boutsb", [1, D], F32)
    c0row = p.sb("c0row", [1, D], F32)
    W1b = p.sb("W1b", [128, 1024], F32)
    B1b = p.sb("B1b", [128, 1024], F32)
    W2b = p.sb("W2b", [128, 1024], F32)
    Binb = p.sb("Binb", [128, 64], F32)
    C0b = p.sb("C0b", [128, 1024], F32)
    winb = p.sb("winb", [128, 8 * 64], BF16)
    woutb = p.sb("woutb", [64, D], BF16)
    b2bf = p.sb("b2bf", [64, 1], BF16)
    s_sb = p.sb("s_sb", [128, NT * 64], F32)
    gbf = p.sb("gbf", [128, 64], BF16)
    gT = p.sb("gT", [64, NT * 128], BF16)
    ones1f = p.sb("ones1f", [1, 128], F32)

    psT = p.ps("psT", [128, 2048], BF16)
    psP = p.ps("psP", [128, 1024], F32)
    psX = p.ps("psX", [128, 1024], F32)
    psG = p.ps("psG", [64, 128], BF16)

    sload = p.sem("sload")
    smm = p.sem("smm")
    sev = p.sem("sev")
    sst = p.sem("sst")

    with nc.Block() as b:
        @b.sync
        def _(e):
            sload.inc(e.dma_start(idsb[:], ident[:]), 16)
            sload.inc(e.dma_start(rows[0:1, 0:64], binr[:]), 16)
            sload.inc(e.dma_start(rows[0:1, 1024:2048], w1r[:]), 16)
            sload.inc(e.dma_start(rows[0:1, 2048:3072], b1r[:]), 16)
            sload.inc(e.dma_start(rows[0:1, 3072:4096], w2r[:]), 16)
            sload.inc(e.dma_start(boutsb[:], boutr[:]), 16)

        @b.gpsimd
        def _(e):
            emit_wload(e, win, D, 0, 64, winb, sload)
            sload.inc(e.dma_start(woutb[:], wout.ap()), 16)

        @b.vector
        def _(e):
            e.memset(ones1f[:], 1.0)
    with nc.Block() as b:
        @b.sync
        def _(e):
            sload.wait(e)
    emit_load_x(p, xdram, X, sload)
    emit_transpose_fused(p, X, XT, idsb, psT, stg, smm, sev)

    emit_bcast_row(p, rows, 1024, 1024, W1b, 0, ones1f, psX, smm)
    emit_bcast_row(p, rows, 2048, 1024, B1b, 0, ones1f, psX, smm)
    emit_bcast_row(p, rows, 3072, 1024, W2b, 0, ones1f, psX, smm)
    emit_bcast_row(p, rows, 0, 64, Binb, 0, ones1f, psX, smm)

    b2sb = p.sb("b2sb", [64, 1], F32)
    with nc.Block() as b:
        @b.sync
        def _(e):
            sload.inc(e.dma_start(b2sb[:], b2c.ap()), 16)
            sload.wait(e)
    with nc.Block() as b:
        @b.vector
        def _(e):
            e.tensor_copy(b2bf[:], b2sb[:])
    with nc.Block() as b:
        @b.tensor
        def _(e):
            for i in range(2):
                smm.inc(e.matmul(psX[0:1, i * 512:(i + 1) * 512],
                                 b2bf[:], woutb[:, i * 512:(i + 1) * 512],
                                 start=True, stop=True), 1)

        @b.vector
        def _(e):
            smm.wait(e)
            e.tensor_add(c0row[:], psX[0:1, 0:1024], boutsb[:])
    emit_bcast_row(p, c0row, 0, 1024, C0b, 0, ones1f, psX, smm)

    for t in range(NT):
        with nc.Block() as b:
            @b.tensor
            def _(e):
                for dc in range(8):
                    ins = e.matmul(psP[:, 0:64],
                                   XT[:, dc * TOK + t * 128:dc * TOK + t * 128 + 128],
                                   winb[:, dc * 64:(dc + 1) * 64],
                                   start=(dc == 0), stop=(dc == 7))
                smm.inc(ins, 1)

            @b.vector
            def _(e):
                smm.wait(e)
                e.tensor_add(s_sb[:, t * 64:(t + 1) * 64], psP[:, 0:64], Binb[:])

    for t in range(NT):
        with nc.Block() as b:
            @b.vector
            def _(e):
                sexp = bass.AP(s_sb.ap().tensor, s_sb.ap().offset + t * 64,
                               [[NT * 64, 128], [1, 64], [0, 16]])
                w1v = W1b.ap().rearrange("p (m h) -> p m h", h=16)
                e.tensor_tensor(scr.ap().rearrange("p (m h) -> p m h", h=16),
                                sexp, w1v, ALU.mult)
                e.tensor_add(scr[:], scr[:], B1b[:])
                e.tensor_relu(scr[:], scr[:])
                e.tensor_tensor(scr[:], scr[:], W2b[:], ALU.mult)
                for k in range(4):
                    n = 512 >> k
                    dst = scr2 if k % 2 == 0 else scr
                    src = scr if k % 2 == 0 else scr2
                    pitch = src.ap().ap[0][0]
                    a0 = bass.AP(src.ap().tensor, src.ap().offset, [[pitch, 128], [2, n]])
                    a1 = bass.AP(src.ap().tensor, src.ap().offset + 1, [[pitch, 128], [2, n]])
                    e.tensor_tensor(dst[:, 0:n], a0, a1, ALU.add)
                e.tensor_copy(gbf[:], scr[:, 0:64])
                sev.inc(e.engine_nop(), 1)

            @b.tensor
            def _(e):
                sev.wait(e)
                smm.inc(e.transpose(psG[:], gbf[:], idsb[:]), 1)

            @b.vector
            def _(e):
                smm.wait(e)
                e.tensor_copy(gT[:, t * 128:(t + 1) * 128], psG[0:64, :])
        with nc.Block() as b:
            @b.tensor
            def _(e):
                for half in range(2):
                    smm.inc(e.matmul(psP[:, half * 512:half * 512 + 512],
                                     gT[:, t * 128:(t + 1) * 128],
                                     woutb[:, half * 512:half * 512 + 512],
                                     start=True, stop=True), 1)

            @b.vector
            def _(e):
                mmb = smm.val - 2
                for half in range(2):
                    e.wait_ge(smm.sem, mmb + half + 1)
                    sl = slice(t * D + half * 512, t * D + half * 512 + 512)
                    e.tensor_add(X[:, sl], X[:, sl], psP[:, half * 512:half * 512 + 512])
                    e.tensor_add(X[:, sl], X[:, sl], C0b[:, half * 512:half * 512 + 512])
    return emit_postln_store(p, X, xo, scr, sst)



_PROG_CACHE = {}


def _prog(key, builder, *a, **k):
    if key not in _PROG_CACHE:
        _PROG_CACHE[key] = builder(key, *a, **k)
    return _PROG_CACHE[key]


def _run(p, in_maps):
    res = run_bass_kernel_spmd(p.nc, in_maps, core_ids=list(range(NCORES)))
    return res.results


def _f32(a):
    return np.ascontiguousarray(np.asarray(a, dtype=np.float32))


def _gather_attn_inputs(r1):
    kts, vgs, cbts = [], [], []
    for b in range(2):
        rr = [r1[4 * b + r] for r in range(4)]
        kts.append(gather_kt([np.asarray(x["kt"]) for x in rr]))
        vgs.append(gather_v_aug([np.asarray(x["v"]) for x in rr]))
        cbts.append(np.ascontiguousarray(
            np.concatenate([_f32(x["qkn"]) for x in rr], axis=1)))
    return kts, vgs, cbts


def _attn_stage(xs, inp, *, tag, qkv_kwargs, qkv_weights, core_kwargs, wo,
                per_core_extra=None):
    p1 = _prog(tag + "_qkv", build_qkv, **qkv_kwargs)
    in_maps = []
    for c in range(NCORES):
        m = dict(x=xs[c], ident=_IDENT, **qkv_weights)
        if per_core_extra:
            m.update(per_core_extra(c))
        in_maps.append(m)
    r1 = _run(p1, in_maps)
    kts, vgs, cbts = _gather_attn_inputs(r1)
    p2 = _prog(tag + "_core", build_attn_core, **core_kwargs)
    in_maps = []
    for c in range(NCORES):
        b, r = divmod(c, 4)
        m = dict(x=xs[c], qt=np.asarray(r1[c]["qt"]), ktg=kts[b], vg=vgs[b],
                 cbt=cbts[b], wo=wo)
        if core_kwargs.get("causal"):
            m["masks"] = causal_masks_for_core(r)
        in_maps.append(m)
    r2 = _run(p2, in_maps)
    return [_f32(r["xo"]) for r in r2]


def _stage_l0_attn(xs, inp):
    return _attn_stage(
        xs, inp, tag="l0_attn",
        qkv_kwargs=dict(preln=False, affine=False, qw=1024, kw=1024, vw=1024,
                        rope=None),
        qkv_weights=dict(wq=_f32(inp["L0_attn_Wq"]), wk=_f32(inp["L0_attn_Wk"]),
                         wv=_f32(inp["L0_attn_Wv"])),
        core_kwargs=dict(nqh=16, nkh=16, causal=True, res_mult=2.0),
        wo=_f32(inp["L0_attn_Wo"]))


def _stage_l0_ffn(xs, inp):
    p = _prog("l0_ffn", build_l0_ffn)
    W1 = _f32(inp["L0_ffn_W1"])
    b1 = _f32(inp["L0_ffn_b1"])
    W2 = _f32(inp["L0_ffn_W2"])
    in_maps = [dict(x=xs[c], win=_f32(inp["L0_ffn_Win"]),
                    wout=_f32(inp["L0_ffn_Wout"]),
                    binr=_f32(inp["L0_ffn_bin_"]).reshape(1, 64),
                    w1r=W1.reshape(1, 1024), b1r=b1.reshape(1, 1024),
                    w2r=W2.reshape(1, 1024),
                    b2c=_f32(inp["L0_ffn_b2"]).reshape(64, 1),
                    boutr=_f32(inp["L0_ffn_bout"]).reshape(1, D),
                    ident=_IDENT)
               for c in range(NCORES)]
    r = _run(p, in_maps)
    return [_f32(x["xo"]) for x in r]


_STAGES = [
    ("l0_attn", _stage_l0_attn),
    ("l0_ffn", _stage_l0_ffn),
]


def _forward(inputs, upto=None):
    xs = shard_tokens(_f32(inputs["L0_attn_x"]))
    for name, fn in _STAGES:
        xs = fn(xs, inputs)
        if upto is not None and name == upto:
            break
    return xs


def kernel(**inputs):
    xs = _forward(inputs)
    return unshard_tokens(xs, D)




def rope_tables_l1(pos):
    inv = 1.0 / 10000.0 ** (np.arange(0, 64, 2, dtype=np.float32) / 64)
    t = np.arange(SEQ, dtype=np.float32)
    fr = np.outer(t, inv)
    cos = np.concatenate([np.cos(fr), np.cos(fr)], -1)[:, ::2]
    sin = np.concatenate([np.sin(fr), np.sin(fr)], -1)[:, ::2]
    ct = np.ascontiguousarray(np.tile(cos[pos].T, (4, 1)).astype(BF))
    st = np.ascontiguousarray(np.tile(sin[pos].T, (4, 1)).astype(BF))
    return ct, st


def rope_tables_std(pos):
    inv = 1.0 / 10000.0 ** (np.arange(0, 64, 2, dtype=np.float32) / 64)
    t = np.arange(SEQ, dtype=np.float32)
    fr = np.outer(t, inv)
    ct = np.ascontiguousarray(np.tile(np.cos(fr)[pos].T, (4, 1)).astype(BF))
    st = np.ascontiguousarray(np.tile(np.sin(fr)[pos].T, (4, 1)).astype(BF))
    return ct, st


def _stage_l1_attn(xs, inp):
    perm = deinterleave_perm(16)
    wdq = _f32(inp["L1_attn_w_down_q"]).T.copy()
    wdkv = _f32(inp["L1_attn_w_down_kv"]).T.copy()
    wuq = np.ascontiguousarray(_f32(inp["L1_attn_w_up_q"]).T[:, perm])
    wuk = np.ascontiguousarray(_f32(inp["L1_attn_w_up_k"]).T[:, perm])
    wuv = np.ascontiguousarray(_f32(inp["L1_attn_w_up_v"]).T)
    wo = np.ascontiguousarray(_f32(inp["L1_attn_w_out"]).T)
    gam = featmajor_rows(_f32(inp["L1_attn_ln_scale"]))
    bet = featmajor_rows(_f32(inp["L1_attn_ln_bias"]))
    swp = swap_sign_matrix()

    def extra(c):
        ct, st = rope_tables_l1(positions_for_core(c))
        return dict(ctab=ct, stab=st, swp=swp, gam=gam, bet=bet)

    return _attn_stage(
        xs, inp, tag="l1_attn",
        qkv_kwargs=dict(preln=True, affine=True, qw=1024, kw=1024, vw=1024,
                        rope=True, qcompress=dict(cw=512), compress=dict(cw=512)),
        qkv_weights=dict(wq_c=wdq, wq2=wuq, wk_c=wdkv, wk2=wuk,
                         wv_c=wdkv, wv2=wuv),
        core_kwargs=dict(nqh=16, nkh=16, causal=False, res_mult=2.0),
        wo=wo, per_core_extra=extra)


def _stage_l2_attn(xs, inp):
    swp = swap_sign_matrix()

    def extra(c):
        ct, st = rope_tables_std(positions_for_core(c))
        return dict(ctab=ct, stab=st, swp=swp)

    return _attn_stage(
        xs, inp, tag="l2_attn",
        qkv_kwargs=dict(preln=False, affine=False, qw=1024, kw=256, vw=256,
                        rope=True),
        qkv_weights=dict(wq=_f32(inp["L2_attn_Wq"]), wk=_f32(inp["L2_attn_Wk"]),
                         wv=_f32(inp["L2_attn_Wv"])),
        core_kwargs=dict(nqh=16, nkh=4, causal=False, res_mult=1.0),
        wo=_f32(inp["L2_attn_Wo"]), per_core_extra=extra)




def build_l2_ffn(name):
    p = Prog(name)
    nc = p.nc
    xdram = p.din("x", [TOK, D])
    w1 = p.din("w1", [D, 4096])
    b1f = p.din("b1f", [128, 32])
    w2 = p.din("w2", [4096, D])
    b2r = p.din("b2r", [1, D])
    ident = p.din("ident", [128, 128], BF16)
    xo = p.dout("xo", [TOK, D])

    X = p.sb("X", [128, NT * D], F32)
    XT = p.sb("XT", [128, 8 * TOK], BF16)
    OUT = p.sb("OUT", [128, NT * D], F32)
    gbuf = p.sb("gbuf", [128, 4 * TOK], BF16)
    w1b = p.sb("w1b", [128, 8 * 128], BF16)
    w2b = p.sb("w2b", [128, 4 * 1024], BF16)
    b1sb = p.sb("b1sb", [128, 32], F32)
    B2b = p.sb("B2b", [128, D], F32)
    b2sb = p.sb("b2sb", [1, D], F32)
    idsb = p.sb("idsb", [128, 128], BF16)
    scr = p.sb("scr", [128, D], F32)
    ones1f = p.sb("ones1f", [1, 128], F32)
    stgL2 = p.sb("stgL2", [128, 1024], BF16)

    psT = p.ps("psT", [128, 2048], BF16)
    psP = p.ps("psP", [128, 1024], F32)
    psX = p.ps("psX", [128, 1024], F32)

    sload = p.sem("sload")
    smm = p.sem("smm")
    sev = p.sem("sev")
    sex = p.sem("sex")
    swt = p.sem("swt")
    sst = p.sem("sst")

    with nc.Block() as b:
        @b.sync
        def _(e):
            sload.inc(e.dma_start(idsb[:], ident[:]), 16)
            sload.inc(e.dma_start(b1sb[:], b1f[:]), 16)
            sload.inc(e.dma_start(b2sb[:], b2r[:]), 16)

        @b.vector
        def _(e):
            e.memset(ones1f[:], 1.0)
    emit_load_x(p, xdram, X, sload)
    with nc.Block() as b:
        @b.sync
        def _(e):
            sload.wait(e)
    emit_transpose_fused(p, X, XT, idsb, psT, stgL2, smm, sev)
    emit_bcast_row(p, b2sb, 0, D, B2b, 0, ones1f, psX, smm)

    for g4 in range(8):
        for j in range(4):
            mc = g4 * 4 + j
            with nc.Block() as b:
                @b.gpsimd
                def _(e):
                    emit_wload(e, w1, D, mc * 128, 128, w1b, swt)

                @b.tensor
                def _(e):
                    swt.wait(e)
                    exb = sex.val
                    for tg in range(4):
                        ps = psP[:, (tg % 2) * 512:(tg % 2) * 512 + 512]
                        if tg >= 2:
                            e.wait_ge(sex.sem, exb + tg - 1)
                        for dc in range(8):
                            ins = e.matmul(
                                ps, w1b[:, dc * 128:(dc + 1) * 128],
                                XT[:, dc * TOK + tg * 512: dc * TOK + tg * 512 + 512],
                                start=(dc == 0), stop=(dc == 7))
                        smm.inc(ins, 1)

                @b.scalar
                def _(e):
                    mmb = smm.val - 4
                    for tg in range(4):
                        ps = psP[:, (tg % 2) * 512:(tg % 2) * 512 + 512]
                        e.wait_ge(smm.sem, mmb + tg + 1)
                        sex.inc(e.activation(
                            gbuf[:, j * TOK + tg * 512: j * TOK + tg * 512 + 512],
                            ps, AF.Gelu, bias=b1sb[:, mc:mc + 1]), 1)
        with nc.Block() as b:
            @b.gpsimd
            def _(e):
                emit_wload(e, w2, 512, 0, 1024, w2b, swt, row0=g4 * 512)
        for t in range(NT):
            with nc.Block() as b:
                @b.tensor
                def _(e):
                    swt.wait(e)
                    for half in range(2):
                        for j in range(4):
                            ins = e.matmul(
                                psP[:, half * 512:half * 512 + 512],
                                gbuf[:, j * TOK + t * 128: j * TOK + t * 128 + 128],
                                w2b[:, j * 1024 + half * 512: j * 1024 + half * 512 + 512],
                                start=(j == 0), stop=(j == 3))
                        smm.inc(ins, 1)

                @b.vector
                def _(e):
                    mmb = smm.val - 2
                    for half in range(2):
                        e.wait_ge(smm.sem, mmb + half + 1)
                        sl = slice(t * D + half * 512, t * D + half * 512 + 512)
                        ps = psP[:, half * 512:half * 512 + 512]
                        if g4 == 0:
                            e.tensor_copy(OUT[:, sl], ps)
                        else:
                            e.tensor_add(OUT[:, sl], OUT[:, sl], ps)
    with nc.Block() as b:
        @b.vector
        def _(e):
            for t in range(NT):
                sl = slice(t * D, (t + 1) * D)
                e.tensor_add(X[:, sl], X[:, sl], OUT[:, sl])
                e.tensor_add(X[:, sl], X[:, sl], B2b[:])
    return emit_postln_store(p, X, xo, scr, sst)




def build_l1_ffn(name):
    p = Prog(name)
    nc = p.nc
    xdram = p.din("x", [TOK, D])
    ae_r = p.din("ae_r", [4, 512, 32])
    be_r = p.din("be_r", [4, 128, 512])
    ae_i = p.din("ae_i", [4, 512, 32])
    be_i = p.din("be_i", [4, 128, 512])
    brr = p.din("brr", [1, 512])
    bir = p.din("bir", [1, 512])
    ident = p.din("ident", [128, 128], BF16)
    xo = p.dout("xo", [TOK, D])

    X = p.sb("X", [128, NT * D], F32)
    XT = p.sb("XT", [128, 8 * TOK], BF16)
    stg1 = p.sb("stg1", [128, 1024], BF16)
    aesb = p.sb("aesb", [128, 2 * 4 * 4 * 32], F32)
    besb = p.sb("besb", [128, 2 * 4 * 512], F32)
    WT = p.sb("WT", [128, 2 * 4 * 512], F32)
    wtr = p.sb("wtr", [128, 4 * 512], BF16)
    wti = p.sb("wti", [128, 4 * 512], BF16)
    wtin = p.sb("wtin", [128, 4 * 512], BF16)
    Brb = p.sb("Brb", [128, 512], F32)
    Bib = p.sb("Bib", [128, 512], F32)
    rowsb = p.sb("rowsb", [1, 1024], F32)
    idsb = p.sb("idsb", [128, 128], BF16)
    scr = p.sb("scr", [128, D], F32)
    ones1f = p.sb("ones1f", [1, 128], F32)

    psT = p.ps("psT", [128, 2048], BF16)
    psP = p.ps("psP", [128, 1024], F32)
    psX = p.ps("psX", [128, 1024], F32)

    sload = p.sem("sload")
    smm = p.sem("smm")
    sev = p.sem("sev")
    sst = p.sem("sst")

    with nc.Block() as b:
        @b.sync
        def _(e):
            sload.inc(e.dma_start(idsb[:], ident[:]), 16)
            sload.inc(e.dma_start(rowsb[0:1, 0:512], brr[:]), 16)
            sload.inc(e.dma_start(rowsb[0:1, 512:1024], bir[:]), 16)
            for vi, (ae, be) in enumerate(((ae_r, be_r), (ae_i, be_i))):
                for c in range(4):
                    src = ae.ap()[c].rearrange("(cc p) n -> p cc n", p=128)
                    dst = aesb.ap()[:, (vi * 4 + c) * 128:(vi * 4 + c) * 128 + 128] \
                        .rearrange("p (cc n) -> p cc n", n=32)
                    sload.inc(e.dma_start(dst, src), 16)
                    sload.inc(e.dma_start(
                        besb[:, (vi * 4 + c) * 512:(vi * 4 + c) * 512 + 512],
                        be.ap()[c]), 16)

        @b.vector
        def _(e):
            e.memset(ones1f[:], 1.0)
    emit_load_x(p, xdram, X, sload)
    with nc.Block() as b:
        @b.sync
        def _(e):
            sload.wait(e)
    emit_transpose_fused(p, X, XT, idsb, psT, stg1, smm, sev)
    emit_bcast_row(p, rowsb, 0, 512, Brb, 0, ones1f, psX, smm)
    emit_bcast_row(p, rowsb, 512, 512, Bib, 0, ones1f, psX, smm)

    with nc.Block() as b:
        @b.vector
        def _(e):
            for vi in range(2):
                for cc in range(4):
                    wsl = slice((vi * 4 + cc) * 512, (vi * 4 + cc) * 512 + 512)
                    for c in range(4):
                        ai = (vi * 4 + c) * 128 + cc * 32
                        aap = bass.AP(aesb.ap().tensor,
                                      aesb.ap().offset + ai,
                                      [[2 * 4 * 4 * 32, 128], [1, 32], [0, 16]])
                        bsl = besb[:, (vi * 4 + c) * 512:(vi * 4 + c) * 512 + 512]
                        if c == 0:
                            e.tensor_tensor(
                                WT.ap()[:, wsl].rearrange("p (a b) -> p a b", b=16),
                                aap, bsl.rearrange("p (a b) -> p a b", b=16),
                                ALU.mult)
                        else:
                            e.tensor_tensor(
                                scr.ap()[:, 0:512].rearrange("p (a b) -> p a b", b=16),
                                aap, bsl.rearrange("p (a b) -> p a b", b=16),
                                ALU.mult)
                            e.tensor_add(WT[:, wsl], WT[:, wsl], scr[:, 0:512])
    with nc.Block() as b:
        @b.vector
        def _(e):
            e.tensor_copy(wtr[:], WT[:, 0:2048])
            e.tensor_copy(wti[:], WT[:, 2048:4096])
            e.tensor_scalar_mul(wtin[:], WT[:, 2048:4096], -1.0)

    for t in range(NT):
        with nc.Block() as b:
            @b.tensor
            def _(e):
                for nc_ in range(4):
                    e.matmul(psP[:, 0:512],
                             XT[:, nc_ * TOK + t * 128: nc_ * TOK + t * 128 + 128],
                             wtr[:, nc_ * 512:(nc_ + 1) * 512],
                             start=(nc_ == 0), stop=False)
                for nc_ in range(4):
                    ins = e.matmul(psP[:, 0:512],
                                   XT[:, (4 + nc_) * TOK + t * 128: (4 + nc_) * TOK + t * 128 + 128],
                                   wtin[:, nc_ * 512:(nc_ + 1) * 512],
                                   start=False, stop=(nc_ == 3))
                smm.inc(ins, 1)
                for nc_ in range(4):
                    e.matmul(psP[:, 512:1024],
                             XT[:, nc_ * TOK + t * 128: nc_ * TOK + t * 128 + 128],
                             wti[:, nc_ * 512:(nc_ + 1) * 512],
                             start=(nc_ == 0), stop=False)
                for nc_ in range(4):
                    ins = e.matmul(psP[:, 512:1024],
                                   XT[:, (4 + nc_) * TOK + t * 128: (4 + nc_) * TOK + t * 128 + 128],
                                   wtr[:, nc_ * 512:(nc_ + 1) * 512],
                                   start=False, stop=(nc_ == 3))
                smm.inc(ins, 1)

            @b.vector
            def _(e):
                mmb = smm.val - 2
                for half, bb in ((0, Brb), (1, Bib)):
                    e.wait_ge(smm.sem, mmb + half + 1)
                    sl = slice(t * D + half * 512, t * D + half * 512 + 512)
                    e.tensor_add(X[:, sl], X[:, sl],
                                 psP[:, half * 512:half * 512 + 512])
                    e.tensor_add(X[:, sl], X[:, sl], bb[:])
    return emit_postln_store(p, X, xo, scr, sst)


def _stage_l1_ffn(xs, inp):
    p = _prog("l1_ffn", build_l1_ffn)
    G1R = _f32(inp["L1_ffn_G1_R"])
    G2R = _f32(inp["L1_ffn_G2_R"])
    G1I = _f32(inp["L1_ffn_G1_I"])
    G2I = _f32(inp["L1_ffn_G2_I"])

    def mk_ae(G1):
        return np.ascontiguousarray(np.stack(
            [np.repeat(G1[0][:, c, :].T, 16, axis=0) for c in range(4)]))

    def mk_be(G2):
        return np.ascontiguousarray(np.stack(
            [np.tile(G2[c, :, 0, :].T, (8, 32)) for c in range(4)]))

    in_maps = [dict(x=xs[c], ae_r=mk_ae(G1R), be_r=mk_be(G2R),
                    ae_i=mk_ae(G1I), be_i=mk_be(G2I),
                    brr=_f32(inp["L1_ffn_bias_real"]).reshape(1, 512),
                    bir=_f32(inp["L1_ffn_bias_imag"]).reshape(1, 512),
                    ident=_IDENT) for c in range(NCORES)]
    return [_f32(r["xo"]) for r in _run(p, in_maps)]


def _stage_l2_ffn(xs, inp):
    p = _prog("l2_ffn", build_l2_ffn)
    in_maps = [dict(x=xs[c], w1=_f32(inp["L2_ffn_w1"]),
                    b1f=featmajor_rows(_f32(inp["L2_ffn_b1"])),
                    w2=_f32(inp["L2_ffn_w2"]),
                    b2r=_f32(inp["L2_ffn_b2"]).reshape(1, D),
                    ident=_IDENT) for c in range(NCORES)]
    return [_f32(r["xo"]) for r in _run(p, in_maps)]


def _emit_l3ffn_tail(p, env):
    nc = p.nc
    X = env["X"]; xo = env["xo"]; xdram = env["xdram"]
    scr = env["scr"]; mu = env["mu"]; rstd = env["rstd"]
    sload = env["sload"]; skv = env["skv"]; smm = env["smm"]
    sex = env["sex"]; sev = env["sev"]; sst = env["sst"]
    psS = env["psS"]; psO = env["psO"]; psB = env["psB"]; psT = env["psT"]
    ones1f = env["ones1f"]
    n1r = env["n1r"]; n2r = env["n2r"]; wgwe = env["wgwe"]; wsi = env["wsi"]
    gebias = env["gebias"]
    wso = env["wso"]; w1s = env["w1s"]; wexp = env["wexp"]; ident = env["ident"]
    from contextlib import ExitStack

    W1b = p.sb("W1b", [128, D], F32)
    W2b = p.sb("W2b", [128, D], F32)
    rws = p.sb("rws", [1, 2 * D], F32)
    idsb = p.sb("idsb", [128, 128], BF16)
    glb = p.sb("glb", [128, NT * 16], F32)
    wrt = p.sb("wrt", [128, NT * 8], F32)
    stg3 = p.sb("stg3", [128, 1024], BF16)
    GEb = p.sb("GEb", [128, 16], F32)
    gerow = p.sb("gerow", [1, 16], F32)

    with nc.Block() as b:
        @b.sync
        def _(e):
            sload.inc(e.dma_start(rws[0:1, 0:D], n1r[:]), 16)
            sload.inc(e.dma_start(rws[0:1, D:2 * D], n2r[:]), 16)
            sload.inc(e.dma_start(idsb[:], ident[:]), 16)
            sload.inc(e.dma_start(gerow[:], env["gebias"].ap() if False else gebias[:]), 16)
            sload.wait(e)
    emit_bcast_row(p, rws, 0, D, W1b, 0, ones1f, psB, smm)
    emit_bcast_row(p, rws, D, D, W2b, 0, ones1f, psB, smm)
    emit_bcast_row(p, gerow, 0, 16, GEb, 0, ones1f, psB, smm)

    emit_ln_stats(p, X, mu, rstd, scr, eps=1e-6, rms=True)
    with nc.Block() as b:
        @b.vector
        def _(e):
            for t in range(NT):
                sl = slice(t * D, (t + 1) * D)
                e.tensor_scalar(X[:, sl], X[:, sl], mu[:, t:t + 1],
                                rstd[:, t:t + 1], ALU.subtract, ALU.mult)
                e.tensor_tensor(X[:, sl], X[:, sl], W1b[:], ALU.mult)
    if env.get("dbgffn"):
        d_x1 = p.dout("d_x1", [TOK, D])
        emit_store_x(p, X, d_x1, sst)

    stkX = ExitStack()
    x1T = stkX.enter_context(nc.sbuf_tensor("x1T", [128, 8 * TOK], BF16,
                                            side="right"))
    emit_transpose_fused(p, X, x1T, idsb, psT, stg3, smm, sev)

    wgb = p.sb("wgb", [128, 8 * 16], BF16)
    with nc.Block() as b:
        @b.gpsimd
        def _(e):
            emit_wload(e, wgwe, D, 0, 16, wgb, skv)
    for t in range(NT):
        with nc.Block() as b:
            @b.tensor
            def _(e):
                skv.wait(e)
                for dc in range(8):
                    ins = e.matmul(psS[:, 0:16],
                                   x1T[:, dc * TOK + t * 128: dc * TOK + t * 128 + 128],
                                   wgb[:, dc * 16:(dc + 1) * 16],
                                   start=(dc == 0), stop=(dc == 7))
                smm.inc(ins, 1)

            @b.vector
            def _(e):
                smm.wait(e)
                e.tensor_add(glb[:, t * 16:(t + 1) * 16], psS[:, 0:16], GEb[:])

    def z(t, a, b2):
        return scr[:, t * 64 + a: t * 64 + b2]

    def _vec_level(fn):
        with nc.Block() as b:
            @b.vector
            def _(e):
                for t in range(NT):
                    fn(e, t)

    def _act_level(fn):
        with nc.Block() as b:
            @b.scalar
            def _(e):
                for t in range(NT):
                    fn(e, t)

    def L1(e, t):
        gl = glb[:, t * 16:(t + 1) * 16]
        e.reduce_max(z(t, 1, 2), gl[:, 0:2], axis=AX.X)
        e.tensor_tensor(z(t, 6, 7), gl[:, 1:2], gl[:, 0:1], ALU.is_gt)

    def L2(e, t):
        e.tensor_scalar_mul(z(t, 0, 1), z(t, 1, 2), -1.0)
        e.tensor_scalar(z(t, 7, 8), z(t, 6, 7), -1.0, 1.0, ALU.mult, ALU.add)

    def L3(e, t):
        e.activation(z(t, 2, 4), glb[:, t * 16:t * 16 + 2],
                     AF.Exp, bias=z(t, 0, 1))

    def L4(e, t):
        gl = glb[:, t * 16:(t + 1) * 16]
        e.reduce_sum(z(t, 4, 5), z(t, 2, 4), axis=AX.X)
        e.reduce_max(z(t, 5, 6), z(t, 2, 4), axis=AX.X)
        m1b = bass.AP(scr.ap().tensor, scr.ap().offset + t * 64 + 6,
                      [[1024, 128], [0, 4]])
        m0b = bass.AP(scr.ap().tensor, scr.ap().offset + t * 64 + 7,
                      [[1024, 128], [0, 4]])
        e.scalar_tensor_tensor(z(t, 8, 12), m1b, -1e30, gl[:, 2:6],
                               ALU.mult, ALU.add)
        e.scalar_tensor_tensor(z(t, 12, 16), m0b, -1e30, gl[:, 6:10],
                               ALU.mult, ALU.add)

    def L5(e, t):
        e.reduce_max(z(t, 16, 17), z(t, 8, 16), axis=AX.X)
        e.reciprocal(z(t, 20, 21), z(t, 4, 5))

    def L6(e, t):
        e.tensor_scalar_mul(z(t, 17, 18), z(t, 16, 17), -1.0)
        e.tensor_tensor(z(t, 21, 22), z(t, 5, 6), z(t, 20, 21), ALU.mult)

    def L7(e, t):
        e.activation(z(t, 24, 32), z(t, 8, 16), AF.Exp, bias=z(t, 17, 18))

    def L8(e, t):
        e.reduce_sum(z(t, 18, 19), z(t, 24, 32), axis=AX.X)

    def L9(e, t):
        e.reciprocal(z(t, 19, 20), z(t, 18, 19))

    def L10(e, t):
        e.tensor_tensor(z(t, 22, 23), z(t, 21, 22), z(t, 19, 20), ALU.mult)

    def L11(e, t):
        e.tensor_scalar(z(t, 32, 40), z(t, 24, 32), z(t, 22, 23), None,
                        ALU.mult)

    def L12(e, t):
        e.reduce_max(z(t, 40, 41), z(t, 32, 40), axis=AX.X)

    def L13(e, t):
        e.tensor_scalar(z(t, 48, 56), z(t, 32, 40), z(t, 40, 41), None,
                        ALU.is_ge)

    def L14(e, t):
        e.scalar_tensor_tensor(z(t, 56, 64), z(t, 48, 56), -1e30,
                               z(t, 32, 40), ALU.mult, ALU.add)

    def L15(e, t):
        e.reduce_max(z(t, 41, 42), z(t, 56, 64), axis=AX.X)

    def L16(e, t):
        e.tensor_scalar(z(t, 48, 56), z(t, 32, 40), z(t, 41, 42), None,
                        ALU.is_ge)

    def L17(e, t):
        e.tensor_tensor(wrt[:, t * 8:(t + 1) * 8], z(t, 32, 40),
                        z(t, 48, 56), ALU.mult)

    _vec_level(L1)
    _vec_level(L2)
    _act_level(L3)
    _vec_level(L4)
    _vec_level(L5)
    _vec_level(L6)
    _act_level(L7)
    _vec_level(L8)
    _vec_level(L9)
    _vec_level(L10)
    _vec_level(L11)
    _vec_level(L12)
    _vec_level(L13)
    _vec_level(L14)
    _vec_level(L15)
    _vec_level(L16)
    _vec_level(L17)
    if env.get("dbgffn"):
        d_glb = p.dout("d_glb", [128, NT * 16])
        d_wrt = p.dout("d_wrt", [128, NT * 8])
        d_scr = p.dout("d_scr", [128, D])
        with nc.Block() as b:
            @b.sync
            def _(e):
                sst.inc(e.dma_start(d_glb.ap(), glb[:]), 16)
                sst.inc(e.dma_start(d_wrt.ap(), wrt[:]), 16)
                sst.inc(e.dma_start(d_scr.ap(), scr[:]), 16)
                sst.wait(e)

    wsib = p.sb("wsib", [128, 8 * 128], BF16)
    swg = p.sb("swg", [128, 2 * TOK], BF16)

    def swiglu_up(wsrc, mc, tok0, ntok, dst, dst_col0):
        ntg = ntok // 512
        for part in range(2):
            col0 = mc * 128 if part == 0 else 2048 + mc * 128
            with nc.Block() as b:
                @b.gpsimd
                def _(e):
                    emit_wload(e, wsrc, D, col0, 128, wsib, skv)

                @b.tensor
                def _(e):
                    skv.wait(e)
                    cons = sex if part == 0 else sev
                    consb = cons.val
                    for tg in range(ntg):
                        ps = psS[:, (tg % 2) * 512:(tg % 2) * 512 + 512]
                        if tg >= 2:
                            e.wait_ge(cons.sem, consb + tg - 1)
                        for dc in range(8):
                            ins = e.matmul(
                                ps, wsib[:, dc * 128:(dc + 1) * 128],
                                x1T[:, dc * TOK + tok0 + tg * 512:
                                    dc * TOK + tok0 + tg * 512 + 512],
                                start=(dc == 0), stop=(dc == 7))
                        smm.inc(ins, 1)

                if part == 0:
                    @b.scalar
                    def _(e):
                        mmb = smm.val - ntg
                        for tg in range(ntg):
                            e.wait_ge(smm.sem, mmb + tg + 1)
                            sex.inc(e.activation(
                                swg[:, tg * 512:(tg + 1) * 512],
                                psS[:, (tg % 2) * 512:(tg % 2) * 512 + 512],
                                AF.Silu), 1)
                else:
                    @b.vector
                    def _(e):
                        mmb = smm.val - ntg
                        for tg in range(ntg):
                            e.wait_ge(smm.sem, mmb + tg + 1)
                            sev.inc(e.tensor_copy(swg[:, TOK + tg * 512:
                                                      TOK + (tg + 1) * 512],
                                                  psS[:, (tg % 2) * 512:
                                                      (tg % 2) * 512 + 512]), 1)
        with nc.Block() as b:
            @b.vector
            def _(e):
                e.tensor_tensor(dst[:, dst_col0:dst_col0 + ntok],
                                swg[:, 0:ntok], swg[:, TOK:TOK + ntok],
                                ALU.mult)

    with ExitStack() as stk2:
        wsob = stk2.enter_context(nc.sbuf_tensor("wsob", [128, 16 * 1024], BF16))
        gsT = stk2.enter_context(nc.sbuf_tensor("gsT", [128, 16 * 1024], BF16))
        with nc.Block() as b:
            @b.gpsimd
            def _(e):
                emit_wload(e, wso, 2048, 0, 1024, wsob, skv)
        for half in range(2):
            for mc in range(16):
                swiglu_up(wsi, mc, half * 1024, 1024, gsT, mc * 1024)
            for t8 in range(8):
                t = half * 8 + t8
                with nc.Block() as b:
                    @b.tensor
                    def _(e):
                        skv.wait(e)
                        for dh in range(2):
                            ps = psS[:, dh * 512:(dh + 1) * 512]
                            for gc in range(16):
                                ins = e.matmul(
                                    ps,
                                    gsT[:, gc * 1024 + t8 * 128: gc * 1024 + t8 * 128 + 128],
                                    wsob[:, gc * 1024 + dh * 512: gc * 1024 + dh * 512 + 512],
                                    start=(gc == 0), stop=(gc == 15))
                            smm.inc(ins, 1)

                    @b.vector
                    def _(e):
                        mmb = smm.val - 2
                        for dh in range(2):
                            e.wait_ge(smm.sem, mmb + dh + 1)
                            sl = slice(t * D + dh * 512, t * D + dh * 512 + 512)
                            e.tensor_add(X[:, sl], X[:, sl],
                                         psS[:, dh * 512:(dh + 1) * 512])

    wexb = p.sb("wexb", [128, 8 * 1024], BF16)
    with ExitStack() as stk3:
        hrT = stk3.enter_context(nc.sbuf_tensor("hrT", [128, 8 * TOK], BF16))
        for gh in range(2):
            for mc in range(8):
                swiglu_up(w1s, gh * 8 + mc, 0, TOK, hrT, mc * TOK)
            for e8 in range(8):
                with nc.Block() as b:
                    @b.gpsimd
                    def _(e):
                        src8 = wexp.ap()[e8][gh * 1024:(gh + 1) * 1024, :] \
                            .rearrange("(c pp) n -> pp c n", pp=128)
                        dst8 = wexb.ap().rearrange("pp (c n) -> pp c n", n=D)
                        skv.inc(e.dma_start(dst8, src8), 16)
                for t in range(NT):
                    with nc.Block() as b:
                        @b.tensor
                        def _(e):
                            skv.wait(e)
                            for dh in range(2):
                                ps = psS[:, dh * 512:(dh + 1) * 512]
                                for gc in range(8):
                                    ins = e.matmul(
                                        ps,
                                        hrT[:, gc * TOK + t * 128: gc * TOK + t * 128 + 128],
                                        wexb[:, gc * D + dh * 512: gc * D + dh * 512 + 512],
                                        start=(gc == 0), stop=(gc == 7))
                                smm.inc(ins, 1)

                        @b.vector
                        def _(e):
                            mmb = smm.val - 2
                            for dh in range(2):
                                e.wait_ge(smm.sem, mmb + dh + 1)
                                sl = slice(t * D + dh * 512, t * D + dh * 512 + 512)
                                e.scalar_tensor_tensor(
                                    X[:, sl], psS[:, dh * 512:(dh + 1) * 512],
                                    wrt[:, t * 8 + e8:t * 8 + e8 + 1],
                                    X[:, sl], ALU.mult, ALU.add)
        stkX.close()

    emit_ln_stats(p, X, mu, rstd, scr, eps=1e-6, rms=True)
    with nc.Block() as b:
        @b.vector
        def _(e):
            for t in range(NT):
                sl = slice(t * D, (t + 1) * D)
                e.tensor_scalar(X[:, sl], X[:, sl], mu[:, t:t + 1],
                                rstd[:, t:t + 1], ALU.subtract, ALU.mult)
                e.tensor_tensor(X[:, sl], X[:, sl], W2b[:], ALU.mult)
    with ExitStack() as stk4:
        Xre = stk4.enter_context(nc.sbuf_tensor("Xre", [128, NT * D], F32))
        emit_load_x(p, xdram, Xre, sload)
        with nc.Block() as b:
            @b.vector
            def _(e):
                for t in range(NT):
                    sl = slice(t * D, (t + 1) * D)
                    e.tensor_add(X[:, sl], X[:, sl], Xre[:, sl])
        emit_ln_stats(p, X, mu, rstd, scr)
        return _finish_attn_store(p, X, mu, rstd, xo, sst)




def build_l3_sums(name):
    p = Prog(name)
    nc = p.nc
    xdram = p.din("x", [TOK, D])
    g1 = p.din("g1", [128, 8])
    b1 = p.din("b1", [128, 8])
    wk = p.din("wk", [D, D])
    wv = p.din("wv", [D, D])
    ident = p.din("ident", [128, 128], BF16)
    sig = p.dout("sig", [16, NT, 64, 64])

    X = p.sb("X", [128, NT * D], F32)
    XT = p.sb("XT", [128, 8 * TOK], BF16)
    KSB = p.sb("KSB", [128, NT * D], BF16)
    VSB = p.sb("VSB", [128, NT * D], BF16)
    sigsb = p.sb("sigsb", [64, NT * 64], F32)
    idsb = p.sb("idsb", [128, 128], BF16)
    scr = p.sb("scr", [128, D], F32)
    stg = p.sb("stg", [128, 1024], BF16)
    g1b = p.sb("g1b", [128, 8], F32)
    b1b = p.sb("b1b", [128, 8], F32)
    mu = p.sb("mu", [128, NT], F32)
    rstd = p.sb("rstd", [128, NT], F32)
    wbuf = p.sb("wbuf", [128, 8 * 512], BF16)

    psT = p.ps("psT", [128, 2048], BF16)
    psP = p.ps("psP", [128, 1024], F32)
    psN = p.ps("psN", [64, 1024], F32)

    sload = p.sem("sload")
    smm = p.sem("smm")
    sev = p.sem("sev")
    swt = p.sem("swt")
    sst = p.sem("sst")

    with nc.Block() as b:
        @b.sync
        def _(e):
            sload.inc(e.dma_start(idsb[:], ident[:]), 16)
            sload.inc(e.dma_start(g1b[:], g1[:]), 16)
            sload.inc(e.dma_start(b1b[:], b1[:]), 16)
    emit_load_x(p, xdram, X, sload)
    with nc.Block() as b:
        @b.sync
        def _(e):
            sload.wait(e)
    emit_ln_stats(p, X, mu, rstd, scr)
    emit_transpose_fused(p, X, XT, idsb, psT, stg, smm, sev, mu=mu, rstd=rstd)
    with nc.Block() as b:
        @b.vector
        def _(e):
            for f in range(8):
                e.tensor_scalar(XT[:, f * TOK:(f + 1) * TOK],
                                XT[:, f * TOK:(f + 1) * TOK],
                                g1b[:, f:f + 1], b1b[:, f:f + 1],
                                ALU.mult, ALU.add)

    def tokmajor_to_sbuf(wdram, dst):
        for vg in range(2):
            with nc.Block() as b:
                @b.gpsimd
                def _(e):
                    emit_wload(e, wdram, D, vg * 512, 512, wbuf, swt)
            for t in range(NT):
                with nc.Block() as b:
                    @b.tensor
                    def _(e):
                        swt.wait(e)
                        for dc in range(8):
                            ins = e.matmul(
                                psP[:, 0:512],
                                XT[:, dc * TOK + t * 128: dc * TOK + t * 128 + 128],
                                wbuf[:, dc * 512:(dc + 1) * 512],
                                start=(dc == 0), stop=(dc == 7))
                        smm.inc(ins, 1)

                    @b.vector
                    def _(e):
                        smm.wait(e)
                        e.tensor_copy(dst[:, t * D + vg * 512: t * D + vg * 512 + 512],
                                      psP[:, 0:512])

    tokmajor_to_sbuf(wk, KSB)
    tokmajor_to_sbuf(wv, VSB)

    for h in range(16):
        with nc.Block() as b:
            evb0 = sev.val
            mmb0 = smm.val

            @b.tensor
            def _(e):
                for t in range(NT):
                    if t >= 2:
                        e.wait_ge(sev.sem, evb0 + t - 1)
                    smm.inc(e.matmul(psN[:, (t % 2) * 512:(t % 2) * 512 + 64],
                                     KSB[:, t * D + h * 64: t * D + h * 64 + 64],
                                     VSB[:, t * D + h * 64: t * D + h * 64 + 64],
                                     start=True, stop=True), 1)

            @b.vector
            def _(e):
                for t in range(NT):
                    e.wait_ge(smm.sem, mmb0 + t + 1)
                    sev.inc(e.tensor_copy(sigsb[:, t * 64:(t + 1) * 64],
                                          psN[:, (t % 2) * 512:(t % 2) * 512 + 64]), 1)
        with nc.Block() as b:
            @b.sync
            def _(e):
                dst = sig.ap()[h].rearrange("t k v -> k t v")
                srcv = sigsb.ap().rearrange("k (t v) -> k t v", v=64)
                sst.inc(e.dma_start(dst, srcv), 16)
                sst.wait(e)
    return p


def emit_accum_x(p, xdram, X, sem, width=D):
    with p.nc.Block() as b:
        @b.gpsimd
        def _(e):
            src = xdram.ap().rearrange("(t p) d -> p t d", p=TPT)
            dst = X.ap().rearrange("p (t d) -> p t d", d=width)
            sem.inc(e.dma_start(dst, src, accum_op=ALU.add), 16)
            sem.wait(e)


def build_l3_main(name):
    p = Prog(name)
    nc = p.nc
    xdram = p.din("x", [TOK, D])
    g1 = p.din("g1", [128, 8])
    b1 = p.din("b1", [128, 8])
    g2 = p.din("g2", [128, 8])
    b2 = p.din("b2", [128, 8])
    wq = p.din("wq", [D, D])
    wk = p.din("wk", [D, D])
    wv = p.din("wv", [D, D])
    wo = p.din("wo", [D, D])
    cw1 = p.din("cw1", [3, D, 4096])
    cb1f = p.din("cb1f", [128, 3 * 32])
    cw2 = p.din("cw2", [3, 4096, D])
    cb2f = p.din("cb2f", [128, 3 * 8])
    cb2r = p.din("cb2r", [1, D])
    sigc = p.din("sigc", [16, NBLK, 64, 64])
    tril = p.din("tril", [128, 128], BF16)
    ident = p.din("ident", [128, 128], BF16)
    xo = p.dout("xo", [TOK, D])

    from contextlib import ExitStack
    ystg = p.sb("ystg", [64, TOK], BF16)
    idsb = p.sb("idsb", [128, 128], BF16)
    trsb = p.sb("trsb", [128, 128], BF16)
    scr = p.sb("scr", [128, D], F32)
    stg = p.sb("stg", [128, 1024], BF16)
    gb = p.sb("gb", [128, 4 * 8], F32)
    cb1sb = p.sb("cb1sb", [128, 3 * 32], F32)
    cb2sb = p.sb("cb2sb", [128, 3 * 8], F32)
    B2b = p.sb("B2b", [128, D], F32)
    rows = p.sb("rows", [1, D], F32)
    mu = p.sb("mu", [128, NT], F32)
    rstd = p.sb("rstd", [128, NT], F32)
    Mt = p.sb("Mt", [128, 64], F32)
    Mtb = p.sb("Mtb", [128, 64], BF16)
    PT1 = p.sb("PT1", [128, 128], BF16)
    wbuf = p.sb("wbuf", [128, 32 * 128], BF16)
    ones1f = p.sb("ones1f", [1, 128], F32)

    psT = p.ps("psT", [128, 2048], BF16)
    psP = p.ps("psP", [128, 1024], F32)
    psS = p.ps("psS", [128, 512], F32)
    psY = p.ps("psY", [128, 512], F32)
    psX = p.ps("psX", [128, 1024], F32)

    sload = p.sem("sload")
    smm = p.sem("smm")
    sev = p.sem("sev")
    sex = p.sem("sex")
    swt = p.sem("swt")
    sst = p.sem("sst")

    with nc.Block() as b:
        @b.sync
        def _(e):
            sload.inc(e.dma_start(idsb[:], ident[:]), 16)
            sload.inc(e.dma_start(trsb[:], tril[:]), 16)
            sload.inc(e.dma_start(gb[:, 0:8], g1[:]), 16)
            sload.inc(e.dma_start(gb[:, 8:16], b1[:]), 16)
            sload.inc(e.dma_start(gb[:, 16:24], g2[:]), 16)
            sload.inc(e.dma_start(gb[:, 24:32], b2[:]), 16)
            sload.inc(e.dma_start(cb1sb[:], cb1f[:]), 16)
            sload.inc(e.dma_start(cb2sb[:], cb2f[:]), 16)
            sload.inc(e.dma_start(rows[:], cb2r[:]), 16)

        @b.vector
        def _(e):
            e.memset(ones1f[:], 1.0)
    with nc.Block() as b:
        @b.sync
        def _(e):
            sload.wait(e)
    emit_bcast_row(p, rows, 0, D, B2b, 0, ones1f, psX, smm)

    stkH = ExitStack()
    hT = stkH.enter_context(nc.sbuf_tensor("hT", [128, 8 * TOK], BF16,
                                           side="right"))
    with ExitStack() as stkX0:
        X = stkX0.enter_context(nc.sbuf_tensor("X", [128, NT * D], F32,
                                               side="right"))
        emit_load_x(p, xdram, X, sload)
        emit_ln_stats(p, X, mu, rstd, scr)
        emit_transpose_fused(p, X, hT, idsb, psT, stg, smm, sev, mu=mu, rstd=rstd)
    with nc.Block() as b:
        @b.vector
        def _(e):
            for f in range(8):
                e.tensor_scalar(hT[:, f * TOK:(f + 1) * TOK],
                                hT[:, f * TOK:(f + 1) * TOK],
                                gb[:, f:f + 1], gb[:, 8 + f:9 + f],
                                ALU.mult, ALU.add)

    stkY = ExitStack()
    yT = stkY.enter_context(nc.sbuf_tensor("yT", [128, 8 * TOK], BF16))
    with ExitStack() as stk2:
        QT = stk2.enter_context(nc.sbuf_tensor("QT", [128, 8 * TOK], BF16))
        KTf = stk2.enter_context(nc.sbuf_tensor("KTf", [128, 8 * TOK], BF16))
        VSB = stk2.enter_context(nc.sbuf_tensor("VSB", [128, NT * D], BF16))
        sigbuf = stk2.enter_context(nc.sbuf_tensor("sigbuf",
                                                   [128, NBLK * 64], F32))

        def fm_to_sbuf(wdram, dst):
            for mchunk in range(8):
                with nc.Block() as b:
                    @b.gpsimd
                    def _(e):
                        emit_wload(e, wdram, D, mchunk * 128, 128, wbuf, swt)

                    @b.tensor
                    def _(e):
                        swt.wait(e)
                        evb = sev.val
                        for tg in range(4):
                            ps = psP[:, (tg % 2) * 512:(tg % 2) * 512 + 512]
                            if tg >= 2:
                                e.wait_ge(sev.sem, evb + tg - 1)
                            for dc in range(8):
                                ins = e.matmul(
                                    ps, wbuf[:, dc * 128:(dc + 1) * 128],
                                    hT[:, dc * TOK + tg * 512: dc * TOK + tg * 512 + 512],
                                    start=(dc == 0), stop=(dc == 7))
                            smm.inc(ins, 1)

                    @b.vector
                    def _(e):
                        mmb = smm.val - 4
                        for tg in range(4):
                            ps = psP[:, (tg % 2) * 512:(tg % 2) * 512 + 512]
                            e.wait_ge(smm.sem, mmb + tg + 1)
                            sev.inc(e.tensor_copy(
                                dst[:, mchunk * TOK + tg * 512: mchunk * TOK + tg * 512 + 512],
                                ps), 1)

        def tokm_to_sbuf(wdram, dst):
            for vg in range(2):
                with nc.Block() as b:
                    @b.gpsimd
                    def _(e):
                        emit_wload(e, wdram, D, vg * 512, 512, wbuf, swt)
                for t in range(NT):
                    with nc.Block() as b:
                        @b.tensor
                        def _(e):
                            swt.wait(e)
                            for dc in range(8):
                                ins = e.matmul(
                                    psP[:, 0:512],
                                    hT[:, dc * TOK + t * 128: dc * TOK + t * 128 + 128],
                                    wbuf[:, dc * 512:(dc + 1) * 512],
                                    start=(dc == 0), stop=(dc == 7))
                            smm.inc(ins, 1)

                        @b.vector
                        def _(e):
                            smm.wait(e)
                            e.tensor_copy(
                                dst[:, t * D + vg * 512: t * D + vg * 512 + 512],
                                psP[:, 0:512])

        fm_to_sbuf(wq, QT)
        fm_to_sbuf(wk, KTf)
        tokm_to_sbuf(wv, VSB)

        for h in range(16):
            pb = (h % 2) * 64
            with nc.Block() as b:
                @b.sync
                def _(e):
                    srcv = sigc.ap()[h].rearrange("g k v -> k g v")
                    dstv = sigbuf.ap()[pb:pb + 64, :] \
                        .rearrange("k (g v) -> k g v", v=64)
                    sload.inc(e.dma_start(dstv, srcv), 16)
                    sload.wait(e)
            with nc.Block() as b:
                @b.vector
                def _(e):
                    e.tensor_copy(Mt[pb:pb + 64, :], sigbuf[pb:pb + 64, 0:64])
                    for j in range(1, 4):
                        e.tensor_add(Mt[pb:pb + 64, :], Mt[pb:pb + 64, :],
                                     sigbuf[pb:pb + 64, j * 64:(j + 1) * 64])
                    e.tensor_copy(Mtb[pb:pb + 64, :], Mt[pb:pb + 64, :])
            for t in range(NT):
                with nc.Block() as b:
                    @b.tensor
                    def _(e):
                        smm.inc(e.matmul(
                            psS[:, 0:128],
                            KTf[pb:pb + 64, (h // 2) * TOK + t * 128:
                                (h // 2) * TOK + t * 128 + 128],
                            QT[pb:pb + 64, (h // 2) * TOK + t * 128:
                               (h // 2) * TOK + t * 128 + 128],
                            start=True, stop=True), 1)

                    @b.vector
                    def _(e):
                        smm.wait(e)
                        sev.inc(e.tensor_tensor(PT1[:], psS[:, 0:128], trsb[:],
                                                ALU.mult), 1)

                    @b.tensor
                    def _(e):
                        sev.wait(e)
                        e.matmul(psY[0:64, 0:128], Mtb[pb:pb + 64, :],
                                 QT[pb:pb + 64, (h // 2) * TOK + t * 128:
                                    (h // 2) * TOK + t * 128 + 128],
                                 start=True, stop=False)
                        smm.inc(e.matmul(psY[0:64, 0:128],
                                         VSB[:, t * D + h * 64: t * D + h * 64 + 64],
                                         PT1[:], start=False, stop=True), 1)

                    @b.vector
                    def _(e):
                        smm.wait(e)
                        if h % 2 == 0:
                            dst = yT[0:64, (h // 2) * TOK + t * 128:
                                     (h // 2) * TOK + t * 128 + 128]
                        else:
                            dst = ystg[:, t * 128:(t + 1) * 128]
                        e.tensor_copy(dst, psY[0:64, 0:128])
                        if t < NT - 1:
                            for j in range(4):
                                s0 = (4 + 4 * t + j) * 64
                                e.tensor_add(Mt[pb:pb + 64, :], Mt[pb:pb + 64, :],
                                             sigbuf[pb:pb + 64, s0:s0 + 64])
                            sev.inc(e.tensor_copy(Mtb[pb:pb + 64, :],
                                                  Mt[pb:pb + 64, :]), 1)
            if h % 2 == 1:
                with nc.Block() as b:
                    @b.sync
                    def _(e):
                        dst = yT[64:128, (h // 2) * TOK:(h // 2) * TOK + TOK]
                        sst.inc(e.dma_start(dst, ystg[:]), 16)
                        sst.wait(e)
    stkH.close()

    stkX2 = ExitStack()
    X2 = stkX2.enter_context(nc.sbuf_tensor("X2", [128, NT * D], F32,
                                            side="right"))
    with ExitStack() as stkWO:
        wosb = stkWO.enter_context(nc.sbuf_tensor("wosb", [128, 8 * D], BF16))
        with nc.Block() as b:
            @b.gpsimd
            def _(e):
                emit_wload(e, wo, D, 0, D, wosb, sload)

            @b.sync
            def _(e):
                sload.wait(e)
        for t in range(NT):
            with nc.Block() as b:
                @b.tensor
                def _(e):
                    for half in range(2):
                        for fc in range(8):
                            ins = e.matmul(
                                psP[:, half * 512:half * 512 + 512],
                                yT[:, fc * TOK + t * 128:fc * TOK + t * 128 + 128],
                                wosb[:, fc * D + half * 512:fc * D + half * 512 + 512],
                                start=(fc == 0), stop=(fc == 7))
                        smm.inc(ins, 1)

                @b.vector
                def _(e):
                    mmb = smm.val - 2
                    for half in range(2):
                        e.wait_ge(smm.sem, mmb + half + 1)
                        sl = slice(t * D + half * 512, t * D + half * 512 + 512)
                        e.tensor_copy(X2[:, sl],
                                      psP[:, half * 512:half * 512 + 512])
    emit_accum_x(p, xdram, X2, sload)
    stkY.close()

    with ExitStack() as stkH2:
        hT2 = stkH2.enter_context(nc.sbuf_tensor("hT2", [128, 8 * TOK], BF16,
                                                 side="right"))
        emit_ln_stats(p, X2, mu, rstd, scr)
        emit_transpose_fused(p, X2, hT2, idsb, psT, stg, smm, sev,
                             mu=mu, rstd=rstd)
        with nc.Block() as b:
            @b.vector
            def _(e):
                for f in range(8):
                    e.tensor_scalar(hT2[:, f * TOK:(f + 1) * TOK],
                                    hT2[:, f * TOK:(f + 1) * TOK],
                                    gb[:, 16 + f:17 + f], gb[:, 24 + f:25 + f],
                                    ALU.mult, ALU.add)

        with ExitStack() as stk5:
            gTg = stk5.enter_context(nc.sbuf_tensor("gTg", [128, 32 * 1024],
                                                    BF16))
            for lvl in range(3):
                for half in range(2):
                    tok0 = half * 1024
                    for mc in range(32):
                        with nc.Block() as b:
                            @b.gpsimd
                            def _(e):
                                srcw = cw1.ap()[lvl][:, mc * 128:(mc + 1) * 128] \
                                    .rearrange("(c pp) n -> pp c n", pp=128)
                                dstw = wbuf.ap().rearrange(
                                    "pp (c n) -> pp c n", n=128)[:, 0:8, :]
                                swt.inc(e.dma_start(dstw, srcw), 16)

                            @b.tensor
                            def _(e):
                                swt.wait(e)
                                for tg in range(2):
                                    ps = psP[:, tg * 512:(tg + 1) * 512]
                                    for dc in range(8):
                                        ins = e.matmul(
                                            ps, wbuf[:, dc * 128:(dc + 1) * 128],
                                            hT2[:, dc * TOK + tok0 + tg * 512:
                                                dc * TOK + tok0 + tg * 512 + 512],
                                            start=(dc == 0), stop=(dc == 7))
                                    smm.inc(ins, 1)

                            @b.scalar
                            def _(e):
                                mmb = smm.val - 2
                                for tg in range(2):
                                    e.wait_ge(smm.sem, mmb + tg + 1)
                                    sex.inc(e.activation(
                                        gTg[:, mc * 1024 + tg * 512: mc * 1024 + tg * 512 + 512],
                                        psP[:, tg * 512:(tg + 1) * 512],
                                        AF.Gelu_apprx_tanh,
                                        bias=cb1sb[:, lvl * 32 + mc:
                                                   lvl * 32 + mc + 1]), 1)
                    if lvl < 2:
                        for dc2 in range(8):
                            with nc.Block() as b:
                                @b.gpsimd
                                def _(e):
                                    srcw = cw2.ap()[lvl][:, dc2 * 128:(dc2 + 1) * 128] \
                                        .rearrange("(c pp) n -> pp c n", pp=128)
                                    dstw = wbuf.ap().rearrange(
                                        "pp (c n) -> pp c n", n=128)
                                    swt.inc(e.dma_start(dstw, srcw), 16)

                                @b.tensor
                                def _(e):
                                    swt.wait(e)
                                    for tg in range(2):
                                        ps = psP[:, tg * 512:(tg + 1) * 512]
                                        for gc in range(32):
                                            ins = e.matmul(
                                                ps, wbuf[:, gc * 128:(gc + 1) * 128],
                                                gTg[:, gc * 1024 + tg * 512: gc * 1024 + tg * 512 + 512],
                                                start=(gc == 0), stop=(gc == 31))
                                        smm.inc(ins, 1)

                                @b.vector
                                def _(e):
                                    mmb = smm.val - 2
                                    for tg in range(2):
                                        e.wait_ge(smm.sem, mmb + tg + 1)
                                        e.tensor_scalar(
                                            hT2[:, dc2 * TOK + tok0 + tg * 512:
                                                dc2 * TOK + tok0 + tg * 512 + 512],
                                            psP[:, tg * 512:(tg + 1) * 512],
                                            cb2sb[:, lvl * 8 + dc2:
                                                  lvl * 8 + dc2 + 1],
                                            None, ALU.add)
                    else:
                        for gq in range(8):
                            with nc.Block() as b:
                                @b.gpsimd
                                def _(e):
                                    srcw = cw2.ap()[2][gq * 512:(gq + 1) * 512, :] \
                                        .rearrange("(c pp) n -> pp c n", pp=128)
                                    dstw = wbuf.ap().rearrange(
                                        "pp (c n) -> pp c n", n=D)[:, 0:4, :]
                                    swt.inc(e.dma_start(dstw, srcw), 16)
                            for t8 in range(8):
                                t = half * 8 + t8
                                with nc.Block() as b:
                                    @b.tensor
                                    def _(e):
                                        swt.wait(e)
                                        for dh in range(2):
                                            ps = psP[:, dh * 512:(dh + 1) * 512]
                                            for c4 in range(4):
                                                gc = gq * 4 + c4
                                                ins = e.matmul(
                                                    ps,
                                                    gTg[:, gc * 1024 + t8 * 128: gc * 1024 + t8 * 128 + 128],
                                                    wbuf[:, c4 * D + dh * 512: c4 * D + dh * 512 + 512],
                                                    start=(c4 == 0), stop=(c4 == 3))
                                            smm.inc(ins, 1)

                                    @b.vector
                                    def _(e):
                                        mmb = smm.val - 2
                                        for dh in range(2):
                                            e.wait_ge(smm.sem, mmb + dh + 1)
                                            sl = slice(t * D + dh * 512,
                                                       t * D + dh * 512 + 512)
                                            e.tensor_add(X2[:, sl], X2[:, sl],
                                                         psP[:, dh * 512:(dh + 1) * 512])

    with nc.Block() as b:
        @b.vector
        def _(e):
            for t in range(NT):
                sl = slice(t * D, (t + 1) * D)
                e.tensor_add(X2[:, sl], X2[:, sl], B2b[:])
    emit_accum_x(p, xdram, X2, sload)
    emit_ln_stats(p, X2, mu, rstd, scr)
    return _finish_attn_store(p, X2, mu, rstd, xo, sst)


def _stage_l3_attn(xs, inp):
    p1 = _prog("l3_sums", build_l3_sums)
    g1 = featmajor_rows(_f32(inp["L3_attn_ln1_g"]))
    b1 = featmajor_rows(_f32(inp["L3_attn_ln1_b"]))
    wk = _f32(inp["L3_attn_wk"])
    wv = _f32(inp["L3_attn_wv"])
    in_maps = [dict(x=xs[c], g1=g1, b1=b1, wk=wk, wv=wv, ident=_IDENT)
               for c in range(NCORES)]
    r1 = _run(p1, in_maps)
    sigg = []
    for b in range(2):
        sg = np.zeros((16, NBLK, 64, 64), dtype=np.float32)
        for r in range(4):
            s = _f32(r1[4 * b + r]["sig"])
            sg[:, r::4] = s
        sigg.append(sg)
    zero = np.zeros((64, 64), dtype=np.float32)
    sigcs = []
    for c in range(NCORES):
        b, r = divmod(c, 4)
        sc = np.zeros((16, NBLK, 64, 64), dtype=np.float32)
        for j in range(4):
            if j < r:
                sc[:, j] = sigg[b][:, j]
        for t in range(NT - 1):
            for j in range(4):
                sc[:, 4 + 4 * t + j] = sigg[b][:, 4 * t + r + j]
        sigcs.append(np.ascontiguousarray(sc))

    p2 = _prog("l3_main", build_l3_main)
    tril = np.ascontiguousarray(np.triu(np.ones((128, 128), np.float32)).astype(BF))
    cb1 = _f32(inp["L3_attn_cms_b1"])
    cb2 = _f32(inp["L3_attn_cms_b2"])
    cb1f = np.concatenate([featmajor_rows(cb1[l]) for l in range(3)], axis=1)
    cb2f = np.concatenate([featmajor_rows(cb2[l]) for l in range(3)], axis=1)
    in_maps = []
    for c in range(NCORES):
        b, r = divmod(c, 4)
        in_maps.append(dict(
            x=xs[c], g1=g1, b1=b1,
            g2=featmajor_rows(_f32(inp["L3_attn_ln2_g"])),
            b2=featmajor_rows(_f32(inp["L3_attn_ln2_b"])),
            wq=_f32(inp["L3_attn_wq"]), wk=wk, wv=wv,
            wo=_f32(inp["L3_attn_wo"]),
            cw1=_f32(inp["L3_attn_cms_w1"]), cb1f=cb1f,
            cw2=_f32(inp["L3_attn_cms_w2"]), cb2f=cb2f,
            cb2r=cb2[2].reshape(1, D),
            sigc=sigcs[c], tril=tril, ident=_IDENT))
    r2 = _run(p2, in_maps)
    return [_f32(r["xo"]) for r in r2]


def _stage_l3_ffn(xs, inp):
    perm = deinterleave_perm(16)
    wq = np.ascontiguousarray(_f32(inp["L3_ffn_Wq"])[:, perm])
    wk2 = np.ascontiguousarray(_f32(inp["L3_ffn_Wk"])[:, perm])
    swp = swap_sign_matrix()
    p1 = _prog("l3f_qkv", build_qkv, preln=False, affine=False,
               qw=1024, kw=1024, vw=1024, rope=True, compress=dict(cw=256))
    in_maps = []
    for c in range(NCORES):
        ct, st = rope_tables_std(positions_for_core(c))
        in_maps.append(dict(x=xs[c], ident=_IDENT, wq=wq,
                            wk_c=_f32(inp["L3_ffn_Wk_c"]), wk2=wk2,
                            wv_c=_f32(inp["L3_ffn_Wv_c"]),
                            wv2=_f32(inp["L3_ffn_Wv"]),
                            ctab=ct, stab=st, swp=swp))
    r1 = _run(p1, in_maps)
    kts, vgs, cbts = _gather_attn_inputs(r1)
    p2 = _prog("l3f_core", build_attn_core, nqh=16, nkh=16, causal=False,
               res_mult=1.0, tail="l3ffn")
    wgwe = np.zeros((D, 16), dtype=np.float32)
    wgwe[:, 0:2] = _f32(inp["L3_ffn_Wg_gate"])
    wgwe[:, 2:10] = _f32(inp["L3_ffn_We_gate"])
    gb_ = _f32(inp["L3_ffn_group_bias"])
    eb_ = _f32(inp["L3_ffn_expert_bias"])
    gebias = np.zeros((1, 16), dtype=np.float32)
    gebias[0, 0:2] = gb_
    gebias[0, 2:10] = eb_
    in_maps = []
    for c in range(NCORES):
        b, r = divmod(c, 4)
        in_maps.append(dict(
            x=xs[c], qt=np.asarray(r1[c]["qt"]), ktg=kts[b], vg=vgs[b],
            cbt=cbts[b], wo=_f32(inp["L3_ffn_Wo"]),
            n1r=_f32(inp["L3_ffn_norm1_w"]).reshape(1, D),
            n2r=_f32(inp["L3_ffn_norm2_w"]).reshape(1, D),
            wgwe=wgwe, gebias=gebias,
            wsi=_f32(inp["L3_ffn_W_shared_in"]),
            wso=_f32(inp["L3_ffn_W_shared_out"]),
            w1s=_f32(inp["L3_ffn_W1_shared"]),
            wexp=_f32(inp["L3_ffn_W2_experts"]),
            ident=_IDENT))
    r2 = _run(p2, in_maps)
    return [_f32(r["xo"]) for r in r2]


_STAGES.extend([
    ("l1_attn", _stage_l1_attn),
    ("l1_ffn", _stage_l1_ffn),
    ("l2_attn", _stage_l2_attn),
    ("l2_ffn", _stage_l2_ffn),
    ("l3_attn", _stage_l3_attn),
    ("l3_ffn", _stage_l3_ffn),
])
